# version 05
# baseline (speedup 1.0000x reference)
import math
import numpy as np
from contextlib import ExitStack

import concourse.bass as bass
import concourse.mybir as mybir
from concourse.bass_utils import run_bass_kernel_spmd

F32 = mybir.dt.float32
BF16 = mybir.dt.bfloat16
I32 = mybir.dt.int32
AF = mybir.ActivationFunctionType
ALU = mybir.AluOpType
AX = mybir.AxisListType

NCORES = 8
T = 1024
D = 2048
DI = 8192
KC = D // 128
NH, NKV, HD = 16, 8, 128
THETA = 1e5
ISQ = 1.0 / math.sqrt(128.0)
NEG = -1.0e9

ENGS = ("pe", "dve", "act", "pool", "sync")


class Prog:

    def __init__(self, nc):
        self.nc = nc
        self.thunks = {e: [] for e in ENGS}
        self.sem_handles = {}
        self.cnt = {}
        self.last_wait = {e: {} for e in ENGS}
        self.wrote = {}
        self.readers = {}
        self.fences = {e: {} for e in ENGS}
        self.dma_sems = set()
        self._es = ExitStack()

    def sem(self, name):
        if name not in self.sem_handles:
            self.sem_handles[name] = self._es.enter_context(self.nc.semaphore(name))
            self.cnt[name] = 0
        return name

    def fence(self):
        snap = {s: v for s, v in self.cnt.items() if v > 0}
        for e in ENGS:
            self.fences[e] = dict(snap)

    PARANOID = False

    def emit(self, eng, fn, reads=(), writes=(), sem=None, amt=1, tag=""):
        sem = sem or eng
        self.sem(sem)
        if Prog.PARANOID:
            self.fence()
        waits = {}

        def add_wait(s, v):
            if v > 0 and v > waits.get(s, 0):
                waits[s] = v

        for s, v in self.fences[eng].items():
            add_wait(s, v)
        self.fences[eng] = {}
        for r in reads:
            for s, v in self.wrote.get(r, ()):
                add_wait(s, v)
        for r in writes:
            for s, v in self.wrote.get(r, ()):
                add_wait(s, v)
            for s, v in self.readers.get(r, {}).items():
                add_wait(s, v)

        if amt != 1:
            self.dma_sems.add(sem)
        self.cnt[sem] += amt
        val = self.cnt[sem]

        wl = []
        for s, v in sorted(waits.items()):
            if s in self.dma_sems:
                v = self.cnt[s] if s != sem else self.cnt[s] - amt

            if self.last_wait[eng].get(s, 0) < v:
                wl.append((s, v))
                self.last_wait[eng][s] = v

        def thunk(h, fn=fn, wl=wl, sem=sem, amt=amt):
            for s, v in wl:
                h.wait_ge(self.sem_handles[s], v)
            fn(h).then_inc(self.sem_handles[sem], amt)

        self.thunks[eng].append(thunk)

        for r in reads:
            self.readers.setdefault(r, {})[sem] = val
        for r in writes:
            self.wrote[r] = [(sem, val)]
            self.readers[r] = {}
        return val

    def mark_alias(self, news, olds):
        ents = []
        for o in olds:
            ents.extend(self.wrote.get(o, ()))
            ents.extend(self.readers.get(o, {}).items())
        for n in news:
            self.wrote[n] = list(self.wrote.get(n, ())) + ents

    def raw_wait(self, eng, sem, val):
        self.thunks[eng].append(
            lambda h, sem=sem, val=val: h.wait_ge(self.sem_handles[sem], val))


def build_program(debug=()):
    nc = bass.Bass("TRN2", target_bir_lowering=False)

    x_in = nc.dram_tensor("x_in", [T, D], F32, kind="ExternalInput").ap()
    wg_d = nc.dram_tensor("wg_d", [32, 128, KC, 256], F32, kind="ExternalInput").ap()
    wu_d = nc.dram_tensor("wu_d", [32, 128, KC, 256], F32, kind="ExternalInput").ap()
    wd_d = nc.dram_tensor("wd_d", [32, 128, 8, 512], F32, kind="ExternalInput").ap()
    wq_d = nc.dram_tensor("wq_d", [16, 128, KC, 128], F32, kind="ExternalInput").ap()
    wkv_d = nc.dram_tensor("wkv_d", [4, 128, KC, 512], F32, kind="ExternalInput").ap()
    wo_d = nc.dram_tensor("wo_d", [4, 128, KC, 512], F32, kind="ExternalInput").ap()
    conv_in = nc.dram_tensor("conv_in", [1, 32], F32, kind="ExternalInput").ap()
    gz_in = nc.dram_tensor("gz_in", [1, 4], F32, kind="ExternalInput").ap()
    idx_in = nc.dram_tensor("idx_in", [128, 16], I32, kind="ExternalInput").ap()
    y_out = nc.dram_tensor("y_out", [T, D], F32, kind="ExternalOutput").ap()
    cc_in = nc.dram_tensor("cc_in", [2048, 1024], BF16, kind="Internal").ap()
    cc_out = nc.dram_tensor("cc_out", [2048 * NCORES, 1024], BF16,
                            kind="Internal", addr_space="Shared").ap()
    dbg_outs = {}
    for name in debug:
        if name in ("x1", "x2"):
            dbg_outs[name] = nc.dram_tensor("dbg_" + name, [T, D], F32,
                                            kind="ExternalOutput").ap()
    raw_dumps = {}
    for name in debug:
        if name.startswith("raw"):
            raw_dumps[name] = nc.dram_tensor(
                "dbg_" + name, [128, 16384], F32, kind="ExternalOutput").ap()

    es = ExitStack()
    E = es.enter_context

    xres = E(nc.sbuf_tensor("xres", [128, KC * T], F32))
    xn = E(nc.sbuf_tensor("xn", [128, KC * T], BF16))
    A = E(nc.sbuf_tensor("A", [128, 49152], BF16))
    Af = A[:].bitcast(F32)
    C1 = E(nc.sbuf_tensor("C1", [128, 1536], F32))
    C1b = C1[:].bitcast(BF16)
    identb = E(nc.sbuf_tensor("identb", [128, 128], BF16))
    identf = E(nc.sbuf_tensor("identf", [128, 128], F32))
    diagM = E(nc.sbuf_tensor("diagM", [128, 128], BF16))
    oldM = E(nc.sbuf_tensor("oldM", [128, 128], BF16))
    ones_c = E(nc.sbuf_tensor("ones_c", [128, 1], BF16))
    ones_r = E(nc.sbuf_tensor("ones_r", [1, 128], F32))
    convb = E(nc.sbuf_tensor("convb", [128, 32], F32))
    gzb = E(nc.sbuf_tensor("gzb", [128, 4], F32))
    conv_row = E(nc.sbuf_tensor("conv_row", [1, 32], F32))
    gz_row = E(nc.sbuf_tensor("gz_row", [1, 4], F32))
    idx_sb = E(nc.sbuf_tensor("idx_sb", [128, 16], I32))
    rcol = E(nc.sbuf_tensor("rcol", [128, 8], F32))
    swapm = E(nc.sbuf_tensor("swapm", [128, 128], BF16))
    swapt = E(nc.sbuf_tensor("swapt", [128, 128], BF16))
    mx1 = E(nc.sbuf_tensor("mx1", [128, 1], F32))
    mx2 = E(nc.sbuf_tensor("mx2", [128, 1], F32))
    mx3 = E(nc.sbuf_tensor("mx3", [128, 1], F32))
    mx4 = E(nc.sbuf_tensor("mx4", [128, 1], F32))
    mx5 = E(nc.sbuf_tensor("mx5", [128, 1], F32))
    mx6 = E(nc.sbuf_tensor("mx6", [128, 1], F32))
    sm1 = E(nc.sbuf_tensor("sm1", [128, 1], F32))
    sm2 = E(nc.sbuf_tensor("sm2", [128, 1], F32))
    sm3 = E(nc.sbuf_tensor("sm3", [128, 1], F32))
    constv = E(nc.sbuf_tensor("constv", [128, 8], F32))

    PA = E(nc.psum_tensor("PA", [128, 1024], F32))
    PB = E(nc.psum_tensor("PB", [128, 1024], F32))
    PC = E(nc.psum_tensor("PC", [128, 1024], F32))
    PD = E(nc.psum_tensor("PD", [128, 1024], F32))
    PBb = PB[:].bitcast(BF16)
    PCb = PC[:].bitcast(BF16)

    p = Prog(nc)
    for e in ENGS:
        p.sem(e)
    for s in ("wg0", "wg1", "wu0", "wu1", "wd0", "wd1", "wd2",
              "wwp", "wwq", "wo0", "wo1", "ds", "dso0", "dso1", "cc", "hg",
              "dkv", "dsx0", "dsx1"):
        p.sem(s)

    def Ab(off, n):
        return A[:, off:off + n]

    def Afv(off_bf16, n_f32):
        return Af[:, off_bf16 // 2: off_bf16 // 2 + n_f32]

    GSLOT = [0, 4096]
    USLOT = [8192, 12288]
    WDSLOT = [16384, 20480, 24576]
    HSPSLOT = [28672, 32768, 36864]
    ABS_O = 40960
    WK_O = 45056
    LNF_O = 28672
    STG = [28672, 36864]
    KSLOT = [j * 2048 for j in range(9)]
    VSLOT = [18432 + j * 2048 for j in range(9)]
    WP_O = 36864
    WQ_O = 36864
    Q_O = [38912, 39936]
    P_O = 40960
    PT_O = 42112
    COS_O = 45056
    WO_SLOT = [28672, 36864]
    OSTG = [0, 4096]

    def emit_mm_group(items, reads, writes, tag=""):
        def fn(t):
            r = None
            for out_ap, l_ap, r_ap, st, sp in items:
                r = t.matmul(out_ap, l_ap, r_ap, start=st, stop=sp)
            return r
        p.emit("pe", fn, reads=reads, writes=writes, tag=tag)

    def emit_tr_group(items, reads, writes, tag=""):
        def fn(t):
            r = None
            for out_ap, in_ap, id_ap in items:
                r = t.transpose(out_ap, in_ap, id_ap)
            return r
        p.emit("pe", fn, reads=reads, writes=writes, tag=tag)

    NREG = [f"n{k}" for k in range(KC)]

    p.emit("sync", lambda h: h.dma_start(out=conv_row[:], in_=conv_in),
           writes=["conv_row"], sem="ds", amt=16)
    p.emit("sync", lambda h: h.dma_start(out=gz_row[:], in_=gz_in),
           writes=["gz_row"], sem="ds", amt=16)
    p.emit("sync", lambda h: h.dma_start(out=idx_sb[:], in_=idx_in),
           writes=["idx"], sem="ds", amt=16)

    def asel(ap_, rg, fillval, base, cmul, step, cmp=ALU.is_equal, init=1.0):
        p.emit("pool", lambda h: h.memset(ap_, init), writes=[rg])
        p.emit("pool", lambda h: h.affine_select(
            ap_, ap_, pattern=[[step, 128]], compare_op=cmp, fill=fillval,
            base=base, channel_multiplier=cmul), reads=[rg], writes=[rg])

    asel(identb[:], "identb", 0.0, 0, 1, -1)
    asel(identf[:], "identf", 0.0, 0, 1, -1)
    asel(diagM[:], "diagM", 0.0, 0, 1, -1, cmp=ALU.is_ge, init=1.0)
    asel(oldM[:], "oldM", 0.0, -1, -1, 1, cmp=ALU.is_ge, init=1.0)
    asel(swapm[:], "swapm", 0.0, 64, 1, -1, init=1.0)
    asel(swapt[:], "swapt", 0.0, -64, 1, -1, init=-1.0)
    p.emit("pool", lambda h: h.tensor_tensor(
        swapm[:], swapm[:], swapt[:], op=ALU.add),
        reads=["swapm", "swapt"], writes=["swapm"])
    p.emit("pool", lambda h: h.memset(ones_c[:], 1.0), writes=["ones_c"])
    p.emit("pool", lambda h: h.memset(ones_r[:], 1.0), writes=["ones_r"])
    for i, v in enumerate((1e-5, -math.log(THETA) / 64.0, -math.pi, ISQ)):
        ap_ = constv[:, i:i + 1]
        p.emit("pool", lambda h, ap_=ap_, v=v: h.memset(ap_, v),
               writes=["constv"])
        nc.const_aps.register(F32, v, ap_)

    emit_mm_group([(PA[:, 0:32], ones_r[:], conv_row[:], True, True)],
                  reads=["ones_r", "conv_row"], writes=["PA"], tag="bcconv")
    p.emit("act", lambda h: h.activation(convb[:], PA[:, 0:32], AF.Copy),
           reads=["PA"], writes=["convb"])
    emit_mm_group([(PB[:, 0:4], ones_r[:], gz_row[:], True, True)],
                  reads=["ones_r", "gz_row"], writes=["PB"], tag="bcgz")
    p.emit("act", lambda h: h.activation(gzb[:], PB[:, 0:4], AF.Copy),
           reads=["PB"], writes=["gzb"])

    xv = xres[:].rearrange("p (fc t) -> p fc t", t=T)
    for tt in range(8):
        stg = Afv(STG[tt % 2], 2048)
        srg = f"stg{tt % 2}"
        p.emit("sync", lambda h, tt=tt, stg=stg: h.dma_start(
            out=stg, in_=x_in[tt * 128:(tt + 1) * 128, :]),
            writes=[srg], sem=f"dsx{tt % 2}", amt=16)
        items_a, items_b = [], []
        for fc in range(KC):
            dst = PA if fc < 8 else PB
            slot = fc % 8
            (items_a if fc < 8 else items_b).append(
                (dst[:, slot * 128:(slot + 1) * 128],
                 stg[:, fc * 128:(fc + 1) * 128], identf[:]))
        emit_tr_group(items_a, reads=[srg, "identf"], writes=["PA"])
        emit_tr_group(items_b, reads=[srg, "identf"], writes=["PB"])
        p.emit("dve", lambda h, tt=tt: h.tensor_copy(
            xv[:, 0:8, tt * 128:(tt + 1) * 128],
            PA[:].rearrange("p (a b) -> p a b", b=128)),
            reads=["PA"], writes=[f"x{fc}" for fc in range(8)])
        p.emit("dve", lambda h, tt=tt: h.tensor_copy(
            xv[:, 8:16, tt * 128:(tt + 1) * 128],
            PB[:].rearrange("p (a b) -> p a b", b=128)),
            reads=["PB"], writes=[f"x{fc}" for fc in range(8, 16)])

    def emit_ln(tp):
        lnf = Afv(LNF_O, 6144)
        mB = lnf[:, 0:1024]
        rB = lnf[:, 1024:2048]
        mrow = lnf[0:1, 2048:3072]
        srow = lnf[0:1, 3072:4096]
        trow = lnf[0:1, 4096:5120]
        tmp0 = lnf[:, 5120:6144]
        tmp1 = C1[:, 0:1024]
        sq = C1b[:, 0:1024]
        for fc in range(KC):
            p.emit("dve", lambda h, fc=fc: h.tensor_copy(
                xn[:, fc * T:(fc + 1) * T], xres[:, fc * T:(fc + 1) * T]),
                reads=[f"x{fc}"], writes=[f"n{fc}"], tag=f"{tp}cast{fc}")
        for half in range(2):
            items = [(PA[0:1, half * 512:(half + 1) * 512], ones_c[:],
                      xn[:, fc * T + half * 512: fc * T + half * 512 + 512],
                      fc == 0, fc == KC - 1) for fc in range(KC)]
            emit_mm_group(items, reads=["ones_c"] + NREG, writes=["PA"],
                          tag=f"{tp}sum{half}")
        for fc in range(KC):
            p.emit("dve", lambda h, fc=fc: h.tensor_tensor(
                sq, xn[:, fc * T:(fc + 1) * T], xn[:, fc * T:(fc + 1) * T],
                op=ALU.mult),
                reads=[f"n{fc}"], writes=["c1a"], tag=f"{tp}sq{fc}")
            items = [(PB[0:1, half * 512:(half + 1) * 512], ones_c[:],
                      sq[:, half * 512:half * 512 + 512], fc == 0, fc == KC - 1)
                     for half in range(2)]
            emit_mm_group(items, reads=["ones_c", "c1a"], writes=["PB"],
                          tag=f"{tp}ssq{fc}")
        p.emit("dve", lambda h: h.tensor_scalar(
            mrow, PA[0:1, :], 1.0 / D, None, op0=ALU.mult),
            reads=["PA"], writes=["mrow"])
        p.emit("dve", lambda h: h.tensor_scalar(
            srow, PB[0:1, :], 1.0 / D, None, op0=ALU.mult),
            reads=["PB"], writes=["srow"])
        p.emit("dve", lambda h: h.tensor_tensor(trow, mrow, mrow, op=ALU.mult),
               reads=["mrow"], writes=["trow"])
        p.emit("dve", lambda h: h.tensor_tensor(srow, srow, trow, op=ALU.subtract),
               reads=["srow", "trow"], writes=["srow"])
        p.emit("act", lambda h: h.activation(trow, srow, AF.Sqrt, bias=1e-5),
               reads=["srow"], writes=["trow"])
        p.emit("dve", lambda h: h.reciprocal(srow, trow),
               reads=["trow"], writes=["srow"])
        emit_mm_group([(PC[:, 0:512], ones_r[:], mrow[0:1, 0:512], True, True),
                       (PC[:, 512:1024], ones_r[:], mrow[0:1, 512:1024], True, True)],
                      reads=["ones_r", "mrow"], writes=["PC"], tag=tp + "bcm")
        emit_mm_group([(PD[:, 0:512], ones_r[:], srow[0:1, 0:512], True, True),
                       (PD[:, 512:1024], ones_r[:], srow[0:1, 512:1024], True, True)],
                      reads=["ones_r", "srow"], writes=["PD"], tag=tp + "bcr")
        p.emit("act", lambda h: h.activation(mB, PC[:], AF.Copy),
               reads=["PC"], writes=["mB"])
        p.emit("act", lambda h: h.activation(rB, PD[:], AF.Copy),
               reads=["PD"], writes=["rB"])
        for fc in range(KC):
            tm = tmp0 if fc % 2 == 0 else tmp1
            trg = "lntmp" if fc % 2 == 0 else "c1a"
            p.emit("dve", lambda h, fc=fc, tm=tm: h.tensor_tensor(
                tm, xres[:, fc * T:(fc + 1) * T], mB, op=ALU.subtract),
                reads=[f"x{fc}", "mB"], writes=[trg], tag=f"{tp}sub{fc}")
            p.emit("dve", lambda h, fc=fc, tm=tm: h.tensor_tensor(
                xn[:, fc * T:(fc + 1) * T], tm, rB, op=ALU.mult),
                reads=[trg, "rB"], writes=[f"n{fc}"], tag=f"{tp}mul{fc}")

    def emit_ffn(rnd):
        tp = f"f{rnd}_"

        def load_g(cb):
            slot = cb % 2
            dst = Ab(GSLOT[slot], 4096)
            p.emit("pool", lambda h, cb=cb, dst=dst: h.dma_start(
                out=dst.rearrange("p (kc c) -> p kc c", c=256), in_=wg_d[cb]),
                writes=[f"g{slot}"], sem=f"wg{slot}", amt=16, tag=f"{tp}ldg{cb}")

        def load_u(cb):
            slot = cb % 2
            dst = Ab(USLOT[slot], 4096)
            p.emit("pool", lambda h, cb=cb, dst=dst: h.dma_start(
                out=dst.rearrange("p (kc c) -> p kc c", c=256), in_=wu_d[cb]),
                writes=[f"u{slot}"], sem=f"wu{slot}", amt=16, tag=f"{tp}ldu{cb}")

        def load_wd(qd):
            if qd >= 32:
                return
            slot = qd % 3
            dst = Ab(WDSLOT[slot], 4096)
            p.emit("pool", lambda h, qd=qd, dst=dst: h.dma_start(
                out=dst.rearrange("p (ic c) -> p ic c", c=512), in_=wd_d[qd]),
                writes=[f"wd{slot}"], sem=f"wd{slot}", amt=16, tag=f"{tp}ldd{qd}")

        def gate_up_chunk(cc):
            cb, m2 = cc // 2, cc % 2
            gs = Ab(GSLOT[cb % 2], 4096)
            us = Ab(USLOT[cb % 2], 4096)
            for w, ps, nm, wrg in ((gs, PA, "PA", f"g{cb % 2}"),
                                   (us, PB, "PB", f"u{cb % 2}")):
                items = []
                for kc in range(KC):
                    for nb in range(2):
                        items.append((ps[:, nb * 512:(nb + 1) * 512],
                                      w[:, kc * 256 + m2 * 128: kc * 256 + (m2 + 1) * 128],
                                      xn[:, kc * T + nb * 512: kc * T + nb * 512 + 512],
                                      kc == 0, kc == KC - 1))
                emit_mm_group(items, reads=[wrg] + NREG, writes=[nm],
                              tag=f"{tp}gu{cc}")
            sg = C1[:, 0:1024]
            p.emit("act", lambda h, sg=sg: h.activation(sg, PA[:], AF.Silu),
                   reads=["PA"], writes=["c1a"], tag=f"{tp}silu{cc}")
            slot, pos = (cc // 4) % 3, cc % 4
            hdst = Ab(HSPSLOT[slot] + pos * 1024, 1024)
            p.emit("dve", lambda h, hdst=hdst, sg=sg: h.tensor_tensor(
                hdst, sg, PB[:], op=ALU.mult),
                reads=["c1a", "PB"], writes=[f"hsp{slot}_{pos}"],
                tag=f"{tp}hmul{cc}")

        def sparsify_group(gs):
            slot = gs % 3
            hts = [Ab(HSPSLOT[slot] + i * 1024, 1024) for i in range(4)]
            ab = [Ab(ABS_O + i * 1024, 1024) for i in range(4)]
            wk = [Ab(WK_O + i * 1024, 1024) for i in range(3)]
            hr = [f"hsp{slot}_{i}" for i in range(4)]
            for i in range(4):
                p.emit("act", lambda h, a=ab[i], t=hts[i]: h.activation(
                    a, t, AF.Abs),
                    reads=[hr[i]], writes=[f"ab{i}"], tag=f"{tp}abs{gs}_{i}")
            net = [
                (wk[0], ab[0], ab[1], ALU.max, ["ab0", "ab1"], "wk0"),
                (wk[1], ab[2], ab[3], ALU.max, ["ab2", "ab3"], "wk1"),
                (wk[0], wk[0], wk[1], ALU.min, ["wk0", "wk1"], "wk0"),
                (wk[1], ab[0], ab[1], ALU.min, ["ab0", "ab1"], "wk1"),
                (wk[2], ab[2], ab[3], ALU.min, ["ab2", "ab3"], "wk2"),
                (wk[1], wk[1], wk[2], ALU.max, ["wk1", "wk2"], "wk1"),
                (wk[0], wk[0], wk[1], ALU.max, ["wk0", "wk1"], "wk0"),
            ]
            for o, i0, i1, op, rd, wr in net:
                p.emit("dve", lambda h, o=o, i0=i0, i1=i1, op=op: h.tensor_tensor(
                    o, i0, i1, op=op), reads=rd, writes=[wr], tag=f"{tp}net{gs}")
            for i in range(4):
                p.emit("dve", lambda h, a=ab[i]: h.tensor_tensor(
                    a, a, wk[0], op=ALU.is_ge),
                    reads=[f"ab{i}", "wk0"], writes=[f"ab{i}"], tag=f"{tp}ge{gs}")
                p.emit("dve", lambda h, t=hts[i], a=ab[i]: h.tensor_tensor(
                    t, t, a, op=ALU.mult),
                    reads=[hr[i], f"ab{i}"], writes=[hr[i]], tag=f"{tp}msk{gs}")

        def down_sweep(g8):
            s0, s1 = (2 * g8) % 3, (2 * g8 + 1) % 3
            for fc in range(KC):
                qd = 4 * g8 + fc // 4
                wdsl = Ab(WDSLOT[qd % 3], 4096)
                fo = (fc % 4) * 128
                ps, nm = (PC, "PC") if fc % 2 == 0 else (PD, "PD")
                items, hreads = [], []
                for ic in range(8):
                    slot = s0 if ic < 4 else s1
                    pos = ic % 4
                    hsp = Ab(HSPSLOT[slot] + pos * 1024, 1024)
                    hreads.append(f"hsp{slot}_{pos}")
                    for nb in range(2):
                        items.append((ps[:, nb * 512:(nb + 1) * 512],
                                      wdsl[:, ic * 512 + fo: ic * 512 + fo + 128],
                                      hsp[:, nb * 512:nb * 512 + 512],
                                      ic == 0, ic == 7))
                emit_mm_group(items, reads=[f"wd{qd % 3}"] + hreads,
                              writes=[nm], tag=f"{tp}dn{g8}_{fc}")
                p.emit("dve", lambda h, fc=fc, ps=ps: h.scalar_tensor_tensor(
                    xres[:, fc * T:(fc + 1) * T], ps[:], 0.5,
                    xres[:, fc * T:(fc + 1) * T], op0=ALU.mult, op1=ALU.add),
                    reads=[nm, f"x{fc}"], writes=[f"x{fc}"], tag=f"{tp}acc{g8}_{fc}")
                if fc % 4 == 3:
                    load_wd(qd + 3)

        load_g(0); load_u(0); load_g(1); load_u(1)
        load_wd(0); load_wd(1); load_wd(2)
        for cb in range(32):
            gate_up_chunk(2 * cb)
            gate_up_chunk(2 * cb + 1)
            if cb % 2 == 1:
                sparsify_group(cb // 2)
                gs_ = cb // 2
                if rnd == 0:
                    raw_dump(f"raw_g{gs_ // 4}",
                             Ab(HSPSLOT[gs_ % 3], 4096),
                             [f"hsp{gs_ % 3}_{i}" for i in range(4)],
                             col0=(gs_ % 4) * 4096)
                if cb == 1 and rnd == 0:
                    raw_dump("raw_hsp0", Ab(HSPSLOT[0], 4096),
                             [f"hsp0_{i}" for i in range(4)])
                if cb == 3 and rnd == 0:
                    raw_dump("raw_hsp1", Ab(HSPSLOT[1], 4096),
                             [f"hsp1_{i}" for i in range(4)])
                    raw_dump("raw_wd0", Ab(WDSLOT[0], 4096), ["wd0"])
            if cb + 2 < 32:
                load_g(cb + 2)
                load_u(cb + 2)
            if cb >= 4 and (cb - 4) % 4 == 0:
                g8_ = (cb - 4) // 4
                if g8_ == 1 and rnd == 0:
                    raw_dump("raw_s2", Ab(HSPSLOT[2], 4096),
                             [f"hsp2_{i}" for i in range(4)])
                    raw_dump("raw_s0b", Ab(HSPSLOT[0], 4096),
                             [f"hsp0_{i}" for i in range(4)])
                if g8_ == 3 and rnd == 0:
                    raw_dump("raw_wq0", Ab(WDSLOT[0], 4096), ["wd0"])
                    raw_dump("raw_wq1", Ab(WDSLOT[1], 4096), ["wd1"])
                    raw_dump("raw_wq2", Ab(WDSLOT[2], 4096), ["wd2"])
                    raw_dump("raw_hs0", Ab(HSPSLOT[0], 4096),
                             [f"hsp0_{i}" for i in range(4)])
                    raw_dump("raw_hs1", Ab(HSPSLOT[1], 4096),
                             [f"hsp1_{i}" for i in range(4)])
                    raw_dump("raw_hs2", Ab(HSPSLOT[2], 4096),
                             [f"hsp2_{i}" for i in range(4)])
                down_sweep(g8_)
                if cb == 4 and rnd == 0:
                    raw_dump("raw_xg0", xres[:],
                             [f"x{fc}" for fc in range(KC)])
                if cb == 8 and rnd == 0:
                    raw_dump("raw_xg1", xres[:],
                             [f"x{fc}" for fc in range(KC)])
        down_sweep(7)

    def emit_tables():
        cosT = Afv(COS_O, 1024)
        sinT = Afv(COS_O + 2048, 1024)
        inv = C1[0:1, 0:128]
        pos = C1[0:1, 128:1152]
        p.emit("pool", lambda h: h.iota(
            inv, pattern=[[0, 2], [1, 64]], base=0, channel_multiplier=0,
            allow_small_or_imprecise_dtypes=True), writes=["tt2"])
        p.emit("pool", lambda h: h.iota(
            pos, pattern=[[1, 1024]], base=0, channel_multiplier=0,
            allow_small_or_imprecise_dtypes=True), writes=["c1a", "c1hi"])
        p.emit("act", lambda h: h.activation(
            inv, inv, AF.Exp, scale=-math.log(THETA) / 64.0),
            reads=["tt2"], writes=["tt2"])
        p.emit("dve", lambda h: h.tensor_scalar(
            pos, pos, gz_row[0:1, 2:3], None, op0=ALU.add),
            reads=["c1a", "c1hi", "gz_row"], writes=["c1a", "c1hi"])
        emit_mm_group([(PC[:, 0:512], inv, pos[0:1, 0:512], True, True),
                       (PC[:, 512:1024], inv, pos[0:1, 512:1024], True, True)],
                      reads=["tt2", "c1a", "c1hi"], writes=["PC"], tag="angles")
        twopi = 2 * math.pi
        mtmp = C1[:, 512:1536]
        mint = C1[:, 0:1024].bitcast(I32)[:, 0:1024]
        for dst, drg, phase in ((sinT, "sinT", 0.0), (cosT, "cosT", math.pi / 2)):
            p.emit("dve", lambda h, dst=dst, phase=phase: h.tensor_scalar(
                dst, PC[:], phase, None, op0=ALU.add),
                reads=["PC"], writes=[drg])
            p.emit("dve", lambda h, dst=dst: h.tensor_scalar(
                mtmp, dst, 1.0 / twopi, None, op0=ALU.mult),
                reads=[drg], writes=["c1hi"])
            p.emit("dve", lambda h: h.tensor_copy(mint, mtmp),
                   reads=["c1hi"], writes=["c1a"])
            p.emit("dve", lambda h: h.tensor_copy(mtmp, mint),
                   reads=["c1a"], writes=["c1hi"])
            p.emit("dve", lambda h, dst=dst: h.scalar_tensor_tensor(
                dst, mtmp, -twopi, dst, op0=ALU.mult, op1=ALU.add),
                reads=["c1hi", drg], writes=[drg])
            p.emit("dve", lambda h, dst=dst: h.tensor_scalar(
                mtmp, dst, math.pi, None, op0=ALU.is_gt),
                reads=[drg], writes=["c1hi"])
            p.emit("dve", lambda h, dst=dst: h.scalar_tensor_tensor(
                dst, mtmp, -twopi, dst, op0=ALU.mult, op1=ALU.add),
                reads=["c1hi", drg], writes=[drg])
            p.emit("act", lambda h, dst=dst: h.activation(dst, dst, AF.Sin),
                   reads=[drg], writes=[drg])

    def emit_attention():
        cosT = Afv(COS_O, 1024)
        sinT = Afv(COS_O + 2048, 1024)
        kraw = C1b[:, 2048:3072]
        tt2 = C1[:, 0:512]

        emit_tables()

        wp_s = Ab(WP_O, 8192)
        for pan in range(4):
            p.emit("pool", lambda h, pan=pan: h.dma_start(
                out=wp_s.rearrange("p (kc c) -> p kc c", c=512), in_=wkv_d[pan]),
                writes=["wwp"], sem="wwp", amt=16, tag=f"ldwkv{pan}")
            for sub in range(4):
                ch = pan * 4 + sub
                ps, nm = (PA, "PA") if ch % 2 == 0 else (PB, "PB")
                items = []
                for kc in range(KC):
                    for nb in range(2):
                        items.append((ps[:, nb * 512:(nb + 1) * 512],
                                      wp_s[:, kc * 512 + sub * 128: kc * 512 + (sub + 1) * 128],
                                      xn[:, kc * T + nb * 512: kc * T + nb * 512 + 512],
                                      kc == 0, kc == KC - 1))
                emit_mm_group(items, reads=["wwp"] + NREG, writes=[nm],
                              tag=f"kv{ch}")
                if ch < 8:
                    j = ch
                    kown = Ab(KSLOT[j] + 1024, 1024)
                    p.emit("act", lambda h, ps=ps: h.activation(
                        kraw, ps[:], AF.Copy),
                        reads=[nm], writes=["c1hi"], tag=f"kevac{j}")
                    for nb in range(2):
                        emit_mm_group([(PD[:, nb * 512:(nb + 1) * 512], swapm[:],
                                        kraw[:, nb * 512:nb * 512 + 512], True, True)],
                                      reads=["c1hi", "swapm"], writes=["PD"],
                                      tag=f"kswap{j}")
                        p.emit("dve", lambda h, nb=nb: h.tensor_tensor(
                            PD[:, nb * 512:(nb + 1) * 512],
                            PD[:, nb * 512:(nb + 1) * 512],
                            sinT[:, nb * 512:nb * 512 + 512], op=ALU.mult),
                            reads=["PD", "sinT"], writes=["PD"], tag=f"ksin{j}")
                        p.emit("dve", lambda h, nb=nb: h.tensor_tensor(
                            tt2, kraw[:, nb * 512:nb * 512 + 512],
                            cosT[:, nb * 512:nb * 512 + 512], op=ALU.mult),
                            reads=["c1hi", "cosT"], writes=["tt2"], tag=f"kcos{j}")
                        p.emit("dve", lambda h, kown=kown, nb=nb: h.tensor_tensor(
                            kown[:, nb * 512:nb * 512 + 512],
                            PD[:, nb * 512:(nb + 1) * 512], tt2, op=ALU.add),
                            reads=["PD", "tt2"], writes=[f"kown{j}"], tag=f"krot{j}")
                    p.emit("sync", lambda h, j=j, kown=kown: h.dma_start(
                        out=cc_in[j * 128:(j + 1) * 128, :], in_=kown),
                        reads=[f"kown{j}"], writes=[f"ccin{j}"], sem="dkv", amt=16,
                        tag=f"ctk{j}")
                else:
                    j = ch - 8
                    vown = Ab(VSLOT[j] + 1024, 1024)
                    p.emit("act", lambda h, ps=ps, vown=vown: h.activation(
                        vown, ps[:], AF.Copy),
                        reads=[nm], writes=[f"vown{j}"], tag=f"vevac{j}")
                    p.emit("sync", lambda h, j=j, vown=vown: h.dma_start(
                        out=cc_in[1024 + j * 128: 1024 + (j + 1) * 128, :],
                        in_=vown),
                        reads=[f"vown{j}"], writes=[f"ccinv{j}"], sem="dkv", amt=16,
                        tag=f"ctv{j}")

        p.emit("pool", lambda h: h.collective_compute(
            "AllGather", ALU.bypass, replica_groups=[list(range(NCORES))],
            ins=[cc_in.opt()], outs=[cc_out.opt()]),
            reads=[f"ccin{j}" for j in range(8)] + [f"ccinv{j}" for j in range(8)],
            writes=["ccout"], sem="cc", amt=1, tag="allgather")
        for j in range(8):
            p.emit("pool", lambda h, j=j: h.indirect_dma_start(
                out=Ab(KSLOT[j], 1024), out_offset=None, in_=cc_out,
                in_offset=bass.IndirectOffsetOnAxis(ap=idx_sb[:, j:j + 1], axis=0)),
                reads=["ccout", "idx"], writes=[f"khalo{j}"], sem="hg", amt=16,
                tag=f"gk{j}")
            p.emit("pool", lambda h, j=j: h.indirect_dma_start(
                out=Ab(VSLOT[j], 1024), out_offset=None, in_=cc_out,
                in_offset=bass.IndirectOffsetOnAxis(ap=idx_sb[:, 8 + j:9 + j],
                                                    axis=0)),
                reads=["ccout", "idx"], writes=[f"vhalo{j}"], sem="hg", amt=16,
                tag=f"gv{j}")

        for j in range(7, -1, -1):
            for base, ow, hw, w0c, w1c in ((KSLOT, "kown", "khalo", 0, 8),
                                           (VSLOT, "vown", "vhalo", 16, 24)):
                src = Ab(base[j], 2048)
                dst = Ab(base[j + 1], 2048)
                rd = [f"{ow}{j}", f"{hw}{j}"]
                wdst = [f"{ow}{j}_c"] + ([f"{ow}{j + 1}", f"{hw}{j + 1}"]
                                         if j < 7 else [])
                p.emit("dve", lambda h, src=src: h.tensor_scalar(
                    src[:, 1023:1024], src[:, 1023:1024], gzb[:, 0:1], None,
                    op0=ALU.mult),
                    reads=[rd[1], "gzb"], writes=[rd[1]], tag=f"gz{j}")
                p.emit("dve", lambda h, src=src, dst=dst, c=w1c + j:
                       h.tensor_scalar(dst, src, convb[:, c:c + 1], None,
                                       op0=ALU.mult),
                       reads=rd + ["convb"], writes=wdst, tag=f"cw1{j}")
                p.emit("dve", lambda h, src=src, dst=dst, c=w0c + j:
                       h.scalar_tensor_tensor(
                           dst[:, 1:2048], src[:, 0:2047], convb[:, c:c + 1],
                           dst[:, 1:2048], op0=ALU.mult, op1=ALU.add),
                       reads=rd + ["convb", f"{ow}{j}_c"],
                       writes=[f"{ow}{j}_c"], tag=f"cw0{j}")

        for j in range(8):
            vsl = Ab(VSLOT[j + 1], 2048)
            for half in range(2):
                items = []
                for t8 in range(8):
                    kt = half * 8 + t8
                    items.append((PCb[:, half * 1024 + t8 * 128:
                                      half * 1024 + (t8 + 1) * 128],
                                  vsl[:, kt * 128:(kt + 1) * 128], identb[:]))
                emit_tr_group(items, reads=[f"vown{j}_c", "identb"],
                              writes=["PC"], tag=f"vt{j}_{half}")
                p.emit("dve", lambda h, vsl=vsl, half=half: h.tensor_copy(
                    vsl[:, half * 1024:(half + 1) * 1024],
                    PCb[:, half * 1024:(half + 1) * 1024]),
                    reads=["PC"],
                    writes=[f"vT{j}"] + ([f"vown{j}_c"] if half == 1 else []),
                    tag=f"vte{j}_{half}")

        p.mark_alias(["wwq", "q0", "q1", "P0", "P1", "PT"], ["wwp"])
        p.mark_alias(["PA", "PB2", "PB3", "PC", "PD6", "PD7"],
                     ["PA", "PB", "PC", "PD", "PD2", "PBlo", "PBhi"])
        PDb = PD[:].bitcast(BF16)
        Pbs = [Ab(P_O, 1152), Ab(43264, 1152)]
        PTs = Ab(PT_O, 1152)
        rsumB = C1[:, 512:1536]
        rr2 = C1[0:1, 0:1024]
        wq_s = Ab(WQ_O, 2048)
        SSET = [
            ([PC[:, 0:384], PC[:, 512:896], PD[:, 0:384]],
             ["PC", "PC", "PD6"]),
            ([PA[:, 0:384], PA[:, 512:896], PB[:, 512:896]],
             ["PA", "PA", "PB3"]),
        ]
        SUMAP = [PD[0:1, 384:512], PB[0:1, 896:1024]]
        SUMRG = ["PD6", "PB3"]
        PT9b = PDb[:, 1408:1536]
        PVAP = PD[:, 512:640]

        for h_ in range(NH):
            kv = h_ // 2
            ksl = Ab(KSLOT[kv + 1], 2048)
            vsl = Ab(VSLOT[kv + 1], 2048)
            qslot = Ab(Q_O[h_ % 2], 1024)
            qrg = f"q{h_ % 2}"
            p.emit("pool", lambda h, h_=h_: h.dma_start(
                out=wq_s.rearrange("p (kc c) -> p kc c", c=128), in_=wq_d[h_]),
                writes=["wwq"], sem="wwq", amt=16, tag=f"ldwq{h_}")
            items = []
            for kc in range(KC):
                for nb in range(2):
                    items.append((PA[:, nb * 512:(nb + 1) * 512],
                                  wq_s[:, kc * 128:(kc + 1) * 128],
                                  xn[:, kc * T + nb * 512: kc * T + nb * 512 + 512],
                                  kc == 0, kc == KC - 1))
            emit_mm_group(items, reads=["wwq"] + NREG, writes=["PA"],
                          tag=f"qmm{h_}")
            p.emit("act", lambda h, qslot=qslot: h.activation(
                qslot, PA[:], AF.Copy), reads=["PA"], writes=[qrg],
                tag=f"qevac{h_}")
            for nb in range(2):
                emit_mm_group([(PD[:, nb * 512:(nb + 1) * 512], swapm[:],
                                qslot[:, nb * 512:nb * 512 + 512], True, True)],
                              reads=[qrg, "swapm"], writes=["PD6", "PD7"],
                              tag=f"qswap{h_}")
                p.emit("dve", lambda h, nb=nb: h.tensor_tensor(
                    PD[:, nb * 512:(nb + 1) * 512],
                    PD[:, nb * 512:(nb + 1) * 512],
                    sinT[:, nb * 512:nb * 512 + 512], op=ALU.mult),
                    reads=["PD6", "PD7", "sinT"], writes=["PD6", "PD7"],
                    tag=f"qsin{h_}")
                p.emit("dve", lambda h, qslot=qslot, nb=nb: h.tensor_tensor(
                    tt2, qslot[:, nb * 512:nb * 512 + 512],
                    cosT[:, nb * 512:nb * 512 + 512], op=ALU.mult),
                    reads=[qrg, "cosT"], writes=["tt2"], tag=f"qcos{h_}")
                p.emit("dve", lambda h, qslot=qslot, nb=nb: h.tensor_tensor(
                    qslot[:, nb * 512:nb * 512 + 512],
                    PD[:, nb * 512:(nb + 1) * 512], tt2, op=ALU.add),
                    reads=["PD6", "PD7", "tt2"], writes=[qrg], tag=f"qrot{h_}")

            def emit_S(a, h_=h_, kv=kv, ksl=ksl, qslot=qslot, qrg=qrg):
                segs, rgs = SSET[a % 2]
                for si in range(3):
                    kcols = 128 * a + si * 384
                    emit_mm_group([(segs[si], qslot[:, a * 128:(a + 1) * 128],
                                    ksl[:, kcols:kcols + 384], True, True)],
                                  reads=[qrg, f"kown{kv}_c"], writes=[rgs[si]],
                                  tag=f"s{h_}_{a}_{si}")

            def emit_soft(a, h_=h_):
                segs, rgs = SSET[a % 2]
                Pb = Pbs[a % 2]
                prg = f"P{a % 2}"
                mxs = ((mx1, "mx1"), (mx2, "mx2"), (mx3, "mx3")) if a % 2 == 0 \
                    else ((mx4, "mx4"), (mx5, "mx5"), (mx6, "mx6"))
                m0, m0r = mxs[0]
                for si, (mx, mrg) in enumerate(mxs):
                    p.emit("dve", lambda h, mx=mx, sg=segs[si]: h.tensor_reduce(
                        mx[:], sg, axis=AX.X, op=ALU.max),
                        reads=[rgs[si]], writes=[mrg], tag=f"r{si}{h_}_{a}")
                p.emit("dve", lambda h, m0=m0, m1_=mxs[1][0]: h.tensor_tensor(
                    m0[:], m0[:], m1_[:], op=ALU.max),
                    reads=[m0r, mxs[1][1]], writes=[m0r])
                p.emit("dve", lambda h, m0=m0, m2_=mxs[2][0]: h.tensor_tensor(
                    m0[:], m0[:], m2_[:], op=ALU.max),
                    reads=[m0r, mxs[2][1]], writes=[m0r])
                p.emit("dve", lambda h, m0=m0: h.tensor_scalar(
                    m0[:], m0[:], -ISQ, None, op0=ALU.mult),
                    reads=[m0r], writes=[m0r])
                for si in range(3):
                    p.emit("act", lambda h, sg=segs[si], si=si, Pb=Pb, m0=m0:
                           h.activation(Pb[:, si * 384:(si + 1) * 384], sg,
                                        AF.Exp, bias=m0[:], scale=ISQ),
                           reads=[rgs[si], m0r], writes=[prg],
                           tag=f"exp{h_}_{a}_{si}")
                z = 128 * (8 - a)
                p.emit("dve", lambda h, z=z, Pb=Pb: h.tensor_scalar(
                    Pb[:, 0:z], Pb[:, 0:z], gzb[:, 0:1], None, op0=ALU.mult),
                    reads=[prg, "gzb"], writes=[prg], tag=f"z{h_}_{a}")
                p.emit("dve", lambda h, Pb=Pb: h.tensor_tensor(
                    Pb[:, 0:128], Pb[:, 0:128], oldM[:], op=ALU.mult),
                    reads=[prg, "oldM"], writes=[prg], tag=f"om{h_}_{a}")
                p.emit("dve", lambda h, Pb=Pb: h.tensor_tensor(
                    Pb[:, 1024:1152], Pb[:, 1024:1152], diagM[:], op=ALU.mult),
                    reads=[prg, "diagM"], writes=[prg], tag=f"dm{h_}_{a}")

            def emit_tail(a, h_=h_, kv=kv, vsl=vsl):
                Pb = Pbs[a % 2]
                prg = f"P{a % 2}"
                items = [(PBb[:, i * 128:(i + 1) * 128],
                          Pb[:, i * 128:(i + 1) * 128], identb[:])
                         for i in range(8)]
                emit_tr_group(items, reads=[prg, "identb"], writes=["PB2"],
                              tag=f"pt{h_}_{a}")
                emit_tr_group([(PT9b, Pb[:, 1024:1152], identb[:])],
                              reads=[prg, "identb"], writes=["PD7"],
                              tag=f"pt9{h_}_{a}")
                p.emit("act", lambda h: h.activation(
                    PTs[:, 0:1024], PBb[:, 0:1024], AF.Copy),
                    reads=["PB2"], writes=["PT"], tag=f"pte{h_}_{a}")
                p.emit("act", lambda h: h.activation(
                    PTs[:, 1024:1152], PT9b, AF.Copy),
                    reads=["PD7"], writes=["PT"], tag=f"pt9e{h_}_{a}")
                items = [(PVAP, vsl[:, (a + i) * 128:(a + i + 1) * 128],
                          PTs[:, i * 128:(i + 1) * 128], i == 0, i == 8)
                         for i in range(9)]
                emit_mm_group(items, reads=["PT", f"vT{kv}"], writes=["PD7"],
                              tag=f"pv{h_}_{a}")
                items = [(SUMAP[a % 2], ones_c[:],
                          PTs[:, i * 128:(i + 1) * 128], i == 0, i == 8)
                         for i in range(9)]
                emit_mm_group(items, reads=["PT", "ones_c"],
                              writes=[SUMRG[a % 2]], tag=f"sum{h_}_{a}")
                ao = Ab(KSLOT[h_ // 2] + (h_ % 2) * 1024, 1024)
                p.emit("dve", lambda h, ao=ao, a=a: h.tensor_copy(
                    ao[:, a * 128:(a + 1) * 128], PVAP),
                    reads=["PD7"], writes=[f"ao{h_}"], tag=f"aoe{h_}_{a}")
                p.emit("dve", lambda h, a=a: h.reciprocal(
                    rr2[0:1, a * 128:(a + 1) * 128], SUMAP[a % 2]),
                    reads=[SUMRG[a % 2]], writes=["tt2", "c1a"],
                    tag=f"rc{h_}_{a}")

            emit_S(0)
            for a in range(1, 9):
                if a < 8:
                    emit_S(a)
                emit_soft(a - 1)
                emit_tail(a - 1)
            items = [(PA[:, a * 128:(a + 1) * 128], ones_r[:],
                      rr2[0:1, a * 128:(a + 1) * 128], True, True)
                     for a in range(8)]
            emit_mm_group(items, reads=["tt2", "c1a", "ones_r"], writes=["PA"],
                          tag=f"rbc{h_}")
            p.emit("act", lambda h: h.activation(rsumB, PA[:], AF.Copy),
                   reads=["PA"], writes=["c1hi"], tag=f"rbe{h_}")
            ao = Ab(KSLOT[h_ // 2] + (h_ % 2) * 1024, 1024)
            p.emit("dve", lambda h, ao=ao: h.tensor_tensor(
                ao, ao, rsumB, op=ALU.mult),
                reads=[f"ao{h_}", "c1hi"], writes=[f"ao{h_}"], tag=f"aon{h_}")

        p.mark_alias(["wo1"], ["wwq", "q0", "q1", "P0", "P1", "PT"])
        p.mark_alias(["wo0"], ["vT4", "vT5", "vT6", "vT7"])
        p.mark_alias(["PA", "PB", "PC", "PD", "PD2"],
                     ["PA", "PB2", "PB3", "PC", "PD6", "PD7"])
        for pan in range(4):
            slot = pan % 2
            dst = Ab(WO_SLOT[slot], 8192)
            p.emit("pool", lambda h, pan=pan, dst=dst: h.dma_start(
                out=dst.rearrange("p (kc c) -> p kc c", c=512), in_=wo_d[pan]),
                writes=[f"wo{slot}"], sem=f"wo{slot}", amt=16, tag=f"ldwo{pan}")
            for sub in range(4):
                fc = pan * 4 + sub
                ps, nm = (PC, "PC") if fc % 2 == 0 else (PD, "PD")
                items, rds = [], [f"wo{slot}"]
                for hh in range(16):
                    ao = Ab(KSLOT[hh // 2] + (hh % 2) * 1024, 1024)
                    rds.append(f"ao{hh}")
                    for nb in range(2):
                        items.append((ps[:, nb * 512:(nb + 1) * 512],
                                      dst[:, hh * 512 + sub * 128:
                                          hh * 512 + (sub + 1) * 128],
                                      ao[:, nb * 512:nb * 512 + 512],
                                      hh == 0, hh == 15))
                emit_mm_group(items, reads=rds, writes=[nm], tag=f"wom{fc}")
                p.emit("dve", lambda h, fc=fc, ps=ps: h.tensor_tensor(
                    xres[:, fc * T:(fc + 1) * T],
                    xres[:, fc * T:(fc + 1) * T], ps[:], op=ALU.add),
                    reads=[nm, f"x{fc}"], writes=[f"x{fc}"], tag=f"woacc{fc}")

    def emit_store(dram_ap, tag):
        for tt in range(8):
            stg = Afv(OSTG[tt % 2], 2048)
            org = f"ostg{tt % 2}"
            items_a, items_b = [], []
            for fc in range(KC):
                dst = PA if fc < 8 else PB
                slot = fc % 8
                (items_a if fc < 8 else items_b).append(
                    (dst[:, slot * 128:(slot + 1) * 128],
                     xres[:, fc * T + tt * 128: fc * T + (tt + 1) * 128],
                     identf[:]))
            emit_tr_group(items_a,
                          reads=[f"x{fc}" for fc in range(8)] + ["identf"],
                          writes=["PA"], tag=f"{tag}t{tt}a")
            emit_tr_group(items_b,
                          reads=[f"x{fc}" for fc in range(8, 16)] + ["identf"],
                          writes=["PB"], tag=f"{tag}t{tt}b")
            sv = None
            p.emit("dve", lambda h, stg=stg: h.tensor_copy(
                stg.rearrange("p (a b) -> p a b", b=128)[:, 0:8, :],
                PA[:].rearrange("p (a b) -> p a b", b=128)),
                reads=["PA"], writes=[org])
            p.emit("dve", lambda h, stg=stg: h.tensor_copy(
                stg.rearrange("p (a b) -> p a b", b=128)[:, 8:16, :],
                PB[:].rearrange("p (a b) -> p a b", b=128)),
                reads=["PB"], writes=[org])
            p.emit("sync", lambda h, stg=stg, tt=tt, dram_ap=dram_ap:
                   h.dma_start(out=dram_ap[tt * 128:(tt + 1) * 128, :], in_=stg),
                   reads=[org], writes=[], sem=f"dso{tt % 2}", amt=16,
                   tag=f"{tag}st{tt}")

    def raw_dump(name, ap_, rregs, col0=0):
        if name in raw_dumps:
            p.emit("pool", lambda h, ap_=ap_, name=name, col0=col0: h.dma_start(
                out=raw_dumps[name][:, col0:col0 + ap_.shape[-1]], in_=ap_),
                reads=rregs, sem="hg", amt=16, tag="dump" + name)

    p.fence()
    emit_ln("ln1_")
    raw_dump("raw_xn1", xn[:], NREG)
    p.fence()
    emit_ffn(0)
    p.fence()
    raw_dump("raw_x1f", xres[:], [f"x{fc}" for fc in range(KC)])
    if "x1" in dbg_outs:
        emit_store(dbg_outs["x1"], "dbgx1_")
        p.fence()
    emit_ln("ln2_")
    p.fence()
    emit_attention()
    p.fence()
    if "x2" in dbg_outs:
        emit_store(dbg_outs["x2"], "dbgx2_")
        p.fence()
    emit_ln("ln3_")
    p.fence()
    emit_ffn(1)
    p.fence()
    emit_store(y_out, "out_")
    p.raw_wait("sync", "dso0", p.cnt["dso0"])
    p.raw_wait("sync", "dso1", p.cnt["dso1"])

    with nc.Block() as block:
        @block.sync
        def _(h):
            for t_ in p.thunks["sync"]:
                t_(h)

        @block.tensor
        def _(h):
            for t_ in p.thunks["pe"]:
                t_(h)

        @block.vector
        def _(h):
            for t_ in p.thunks["dve"]:
                t_(h)

        @block.scalar
        def _(h):
            for t_ in p.thunks["act"]:
                t_(h)

        @block.gpsimd
        def _(h):
            for t_ in p.thunks["pool"]:
                t_(h)

    es.close()
    p._es.close()
    return nc



def _perm_cols():
    n = np.arange(DI)
    q = n // 512
    r = (n % 512) // 128
    pp = n % 128
    return 4 * (128 * q + pp) + r


def _prep_weights(w_gate, w_up, w_down, w_pack, w_o):
    perm = _perm_cols()
    wg = np.ascontiguousarray(w_gate[:, perm])
    wu = np.ascontiguousarray(w_up[:, perm])
    wd = np.ascontiguousarray(w_down[perm, :])
    wg_d = np.ascontiguousarray(wg.reshape(KC, 128, 32, 256).transpose(2, 1, 0, 3))
    wu_d = np.ascontiguousarray(wu.reshape(KC, 128, 32, 256).transpose(2, 1, 0, 3))
    wd_d = np.ascontiguousarray(
        wd.reshape(8, 8, 128, 4, 512).transpose(0, 3, 2, 1, 4).reshape(32, 128, 8, 512))
    wq_d = np.ascontiguousarray(
        w_pack[:, :2048].reshape(KC, 128, 16, 128).transpose(2, 1, 0, 3))
    wkv_d = np.ascontiguousarray(
        w_pack[:, 2048:].reshape(KC, 128, 4, 512).transpose(2, 1, 0, 3))
    wo_d = np.ascontiguousarray(
        w_o.reshape(KC, 128, 4, 512).transpose(2, 1, 0, 3))
    return wg_d, wu_d, wd_d, wq_d, wkv_d, wo_d


_nc_cache = {}


def kernel(L0_attn_hidden, L0_attn_W_pack, L0_attn_W_o, L0_attn_conv_k,
           L0_attn_conv_v, L0_ffn_w_gate, L0_ffn_w_up, L0_ffn_w_down,
           _debug=()):
    x = np.ascontiguousarray(np.asarray(L0_attn_hidden, dtype=np.float32))
    w_pack = np.asarray(L0_attn_W_pack, dtype=np.float32)
    w_o = np.asarray(L0_attn_W_o, dtype=np.float32)
    conv_k = np.asarray(L0_attn_conv_k, dtype=np.float32)
    conv_v = np.asarray(L0_attn_conv_v, dtype=np.float32)
    w_gate = np.asarray(L0_ffn_w_gate, dtype=np.float32)
    w_up = np.asarray(L0_ffn_w_up, dtype=np.float32)
    w_down = np.asarray(L0_ffn_w_down, dtype=np.float32)

    wg_d, wu_d, wd_d, wq_d, wkv_d, wo_d = _prep_weights(
        w_gate, w_up, w_down, w_pack, w_o)
    conv_flat = np.ascontiguousarray(np.concatenate(
        [conv_k[:, 0], conv_k[:, 1], conv_v[:, 0], conv_v[:, 1]]
    ).reshape(1, 32).astype(np.float32))

    in_maps = []
    for c in range(NCORES):
        b, t0 = c // 4, (c % 4) * T
        xs = np.ascontiguousarray(x[b, t0:t0 + T, :])
        gamma = 1.0 if (c % 4) else 0.0
        zeta = 0.0 if (c % 4) else NEG
        gz = np.array([[gamma, zeta, float(t0), 0.0]], dtype=np.float32)
        rc = c - 1 if (c % 4) else c
        pidx = np.arange(128, dtype=np.int32)
        cols = [rc * 2048 + j * 128 + pidx for j in range(8)]
        cols += [rc * 2048 + 1024 + j * 128 + pidx for j in range(8)]
        idx = np.ascontiguousarray(np.stack(cols, axis=1).astype(np.int32))
        in_maps.append(dict(
            x_in=xs, wg_d=wg_d, wu_d=wu_d, wd_d=wd_d, wq_d=wq_d,
            wkv_d=wkv_d, wo_d=wo_d, conv_in=conv_flat, gz_in=gz, idx_in=idx))

    key = tuple(sorted(_debug))
    if key not in _nc_cache:
        _nc_cache[key] = build_program(debug=_debug)
    nc = _nc_cache[key]

    res = run_bass_kernel_spmd(nc, in_maps, core_ids=list(range(NCORES)))
    shards = [res.results[c]["y_out"] for c in range(NCORES)]
    out = np.stack(shards, axis=0).reshape(2, 4, T, D).reshape(2, 4096, D)
    if _debug:
        dbg = {}
        for name in _debug:
            d = [res.results[c]["dbg_" + name] for c in range(NCORES)]
            dbg[name] = np.stack(d, axis=0).reshape(2, 4096, D)
        return out, dbg
    return out


# version 06 (final)
# speedup vs baseline: 1.0570x; 1.0570x over previous
import math
import numpy as np
from contextlib import ExitStack

import concourse.bass as bass
import concourse.mybir as mybir
from concourse.bass_utils import run_bass_kernel_spmd

F32 = mybir.dt.float32
BF16 = mybir.dt.bfloat16
I32 = mybir.dt.int32
AF = mybir.ActivationFunctionType
ALU = mybir.AluOpType
AX = mybir.AxisListType

NCORES = 8
T = 1024
D = 2048
DI = 8192
KC = D // 128
NH, NKV, HD = 16, 8, 128
THETA = 1e5
ISQ = 1.0 / math.sqrt(128.0)
NEG = -1.0e9

ENGS = ("pe", "dve", "act", "pool", "sync")


class Prog:

    def __init__(self, nc):
        self.nc = nc
        self.thunks = {e: [] for e in ENGS}
        self.sem_handles = {}
        self.cnt = {}
        self.last_wait = {e: {} for e in ENGS}
        self.wrote = {}
        self.readers = {}
        self.fences = {e: {} for e in ENGS}
        self.dma_sems = set()
        self._es = ExitStack()

    def sem(self, name):
        if name not in self.sem_handles:
            self.sem_handles[name] = self._es.enter_context(self.nc.semaphore(name))
            self.cnt[name] = 0
        return name

    def fence(self):
        snap = {s: v for s, v in self.cnt.items() if v > 0}
        for e in ENGS:
            self.fences[e] = dict(snap)

    PARANOID = False

    def emit(self, eng, fn, reads=(), writes=(), sem=None, amt=1, tag=""):
        sem = sem or eng
        self.sem(sem)
        if Prog.PARANOID:
            self.fence()
        waits = {}

        def add_wait(s, v):
            if v > 0 and v > waits.get(s, 0):
                waits[s] = v

        for s, v in self.fences[eng].items():
            add_wait(s, v)
        self.fences[eng] = {}
        for r in reads:
            for s, v in self.wrote.get(r, ()):
                add_wait(s, v)
        for r in writes:
            for s, v in self.wrote.get(r, ()):
                add_wait(s, v)
            for s, v in self.readers.get(r, {}).items():
                add_wait(s, v)

        if amt != 1:
            self.dma_sems.add(sem)
        self.cnt[sem] += amt
        val = self.cnt[sem]

        wl = []
        for s, v in sorted(waits.items()):
            if s in self.dma_sems:
                v = self.cnt[s] if s != sem else self.cnt[s] - amt

            if self.last_wait[eng].get(s, 0) < v:
                wl.append((s, v))
                self.last_wait[eng][s] = v

        def thunk(h, fn=fn, wl=wl, sem=sem, amt=amt):
            for s, v in wl:
                h.wait_ge(self.sem_handles[s], v)
            fn(h).then_inc(self.sem_handles[sem], amt)

        self.thunks[eng].append(thunk)

        for r in reads:
            self.readers.setdefault(r, {})[sem] = val
        for r in writes:
            self.wrote[r] = [(sem, val)]
            self.readers[r] = {}
        return val

    def mark_alias(self, news, olds):
        ents = []
        for o in olds:
            ents.extend(self.wrote.get(o, ()))
            ents.extend(self.readers.get(o, {}).items())
        for n in news:
            self.wrote[n] = list(self.wrote.get(n, ())) + ents

    def raw_wait(self, eng, sem, val):
        self.thunks[eng].append(
            lambda h, sem=sem, val=val: h.wait_ge(self.sem_handles[sem], val))


def build_program(debug=()):
    nc = bass.Bass("TRN2", target_bir_lowering=False)

    x_in = nc.dram_tensor("x_in", [T, D], F32, kind="ExternalInput").ap()
    wg_d = nc.dram_tensor("wg_d", [32, 128, KC, 256], F32, kind="ExternalInput").ap()
    wu_d = nc.dram_tensor("wu_d", [32, 128, KC, 256], F32, kind="ExternalInput").ap()
    wd_d = nc.dram_tensor("wd_d", [32, 128, 8, 512], F32, kind="ExternalInput").ap()
    wq_d = nc.dram_tensor("wq_d", [16, 128, KC, 128], F32, kind="ExternalInput").ap()
    wkv_d = nc.dram_tensor("wkv_d", [4, 128, KC, 512], F32, kind="ExternalInput").ap()
    wo_d = nc.dram_tensor("wo_d", [4, 128, KC, 512], F32, kind="ExternalInput").ap()
    conv_in = nc.dram_tensor("conv_in", [1, 32], F32, kind="ExternalInput").ap()
    gz_in = nc.dram_tensor("gz_in", [1, 4], F32, kind="ExternalInput").ap()
    idx_in = nc.dram_tensor("idx_in", [128, 16], I32, kind="ExternalInput").ap()
    y_out = nc.dram_tensor("y_out", [T, D], F32, kind="ExternalOutput").ap()
    cc_in = nc.dram_tensor("cc_in", [2048, 1024], BF16, kind="Internal").ap()
    cc_out = nc.dram_tensor("cc_out", [2048 * NCORES, 1024], BF16,
                            kind="Internal", addr_space="Shared").ap()
    dbg_outs = {}
    for name in debug:
        if name in ("x1", "x2"):
            dbg_outs[name] = nc.dram_tensor("dbg_" + name, [T, D], F32,
                                            kind="ExternalOutput").ap()
    raw_dumps = {}
    for name in debug:
        if name.startswith("raw"):
            raw_dumps[name] = nc.dram_tensor(
                "dbg_" + name, [128, 16384], F32, kind="ExternalOutput").ap()

    es = ExitStack()
    E = es.enter_context

    xres = E(nc.sbuf_tensor("xres", [128, KC * T], F32))
    xn = E(nc.sbuf_tensor("xn", [128, KC * T], BF16))
    A = E(nc.sbuf_tensor("A", [128, 49152], BF16))
    Af = A[:].bitcast(F32)
    C1 = E(nc.sbuf_tensor("C1", [128, 1536], F32))
    C1b = C1[:].bitcast(BF16)
    identb = E(nc.sbuf_tensor("identb", [128, 128], BF16))
    identf = E(nc.sbuf_tensor("identf", [128, 128], F32))
    diagM = E(nc.sbuf_tensor("diagM", [128, 128], BF16))
    oldM = E(nc.sbuf_tensor("oldM", [128, 128], BF16))
    ones_c = E(nc.sbuf_tensor("ones_c", [128, 1], BF16))
    ones_r = E(nc.sbuf_tensor("ones_r", [1, 128], F32))
    convb = E(nc.sbuf_tensor("convb", [128, 32], F32))
    gzb = E(nc.sbuf_tensor("gzb", [128, 4], F32))
    conv_row = E(nc.sbuf_tensor("conv_row", [1, 32], F32))
    gz_row = E(nc.sbuf_tensor("gz_row", [1, 4], F32))
    idx_sb = E(nc.sbuf_tensor("idx_sb", [128, 16], I32))
    rcol = E(nc.sbuf_tensor("rcol", [128, 8], F32))
    swapm = E(nc.sbuf_tensor("swapm", [128, 128], BF16))
    swapt = E(nc.sbuf_tensor("swapt", [128, 128], BF16))
    mx1 = E(nc.sbuf_tensor("mx1", [128, 1], F32))
    mx2 = E(nc.sbuf_tensor("mx2", [128, 1], F32))
    mx3 = E(nc.sbuf_tensor("mx3", [128, 1], F32))
    kms = E(nc.sbuf_tensor("kms", [1, 16], F32))
    mrow_s = E(nc.sbuf_tensor("mrow_s", [1, 1024], F32))
    mfm = [E(nc.sbuf_tensor("mfm0", [128, 8], F32)),
           E(nc.sbuf_tensor("mfm1", [128, 8], F32))]
    pt9s = E(nc.sbuf_tensor("pt9s", [128, 128], BF16))
    sm1 = E(nc.sbuf_tensor("sm1", [128, 1], F32))
    sm2 = E(nc.sbuf_tensor("sm2", [128, 1], F32))
    sm3 = E(nc.sbuf_tensor("sm3", [128, 1], F32))
    constv = E(nc.sbuf_tensor("constv", [128, 8], F32))

    PAB = E(nc.psum_tensor("PAB", [128, 2048], F32))
    PCD = E(nc.psum_tensor("PCD", [128, 2048], F32))
    PA = PAB[:, 0:1024]
    PB = PAB[:, 1024:2048]
    PC = PCD[:, 0:1024]
    PD = PCD[:, 1024:2048]
    PABb = PAB[:].bitcast(BF16)
    PCDb = PCD[:].bitcast(BF16)
    PBb = PABb[:, 2048:4096]
    PCb = PCDb[:, 0:2048]

    p = Prog(nc)
    for e in ENGS:
        p.sem(e)
    for s in ("wg0", "wg1", "wu0", "wu1", "wd0", "wd1", "wd2",
              "wwp", "wwq", "wo0", "wo1", "ds", "dso0", "dso1", "cc", "hg",
              "dkv", "dsx0", "dsx1"):
        p.sem(s)

    def Ab(off, n):
        return A[:, off:off + n]

    def Afv(off_bf16, n_f32):
        return Af[:, off_bf16 // 2: off_bf16 // 2 + n_f32]

    GSLOT = [0, 4096]
    USLOT = [8192, 12288]
    WDSLOT = [16384, 20480, 24576]
    HSPSLOT = [28672, 32768, 36864]
    ABS_O = 40960
    WK_O = 45056
    LNF_O = 28672
    STG = [28672, 36864]
    KSLOT = [j * 2048 for j in range(9)]
    VSLOT = [18432 + j * 2048 for j in range(9)]
    WP_O = 36864
    WQ_O = 36864
    Q_O = [38912, 39936]
    P_O = 40960
    PT_O = 42112
    COS_O = 45056
    WO_SLOT = [28672, 36864]
    OSTG = [0, 4096]

    def emit_mm_group(items, reads, writes, tag=""):
        def fn(t):
            r = None
            for out_ap, l_ap, r_ap, st, sp in items:
                r = t.matmul(out_ap, l_ap, r_ap, start=st, stop=sp)
            return r
        p.emit("pe", fn, reads=reads, writes=writes, tag=tag)

    def emit_tr_group(items, reads, writes, tag=""):
        def fn(t):
            r = None
            for out_ap, in_ap, id_ap in items:
                r = t.transpose(out_ap, in_ap, id_ap)
            return r
        p.emit("pe", fn, reads=reads, writes=writes, tag=tag)

    NREG = [f"n{k}" for k in range(KC)]

    p.emit("sync", lambda h: h.dma_start(out=conv_row[:], in_=conv_in),
           writes=["conv_row"], sem="ds", amt=16)
    p.emit("sync", lambda h: h.dma_start(out=gz_row[:], in_=gz_in),
           writes=["gz_row"], sem="ds", amt=16)
    p.emit("sync", lambda h: h.dma_start(out=idx_sb[:], in_=idx_in),
           writes=["idx"], sem="ds", amt=16)

    def asel(ap_, rg, fillval, base, cmul, step, cmp=ALU.is_equal, init=1.0):
        p.emit("pool", lambda h: h.memset(ap_, init), writes=[rg])
        p.emit("pool", lambda h: h.affine_select(
            ap_, ap_, pattern=[[step, 128]], compare_op=cmp, fill=fillval,
            base=base, channel_multiplier=cmul), reads=[rg], writes=[rg])

    asel(identb[:], "identb", 0.0, 0, 1, -1)
    asel(identf[:], "identf", 0.0, 0, 1, -1)
    asel(diagM[:], "diagM", 0.0, 0, 1, -1, cmp=ALU.is_ge, init=1.0)
    asel(oldM[:], "oldM", 0.0, -1, -1, 1, cmp=ALU.is_ge, init=1.0)
    asel(swapm[:], "swapm", 0.0, 64, 1, -1, init=1.0)
    asel(swapt[:], "swapt", 0.0, -64, 1, -1, init=-1.0)
    p.emit("pool", lambda h: h.tensor_tensor(
        swapm[:], swapm[:], swapt[:], op=ALU.add),
        reads=["swapm", "swapt"], writes=["swapm"])
    p.emit("pool", lambda h: h.memset(ones_c[:], 1.0), writes=["ones_c"])
    p.emit("pool", lambda h: h.memset(ones_r[:], 1.0), writes=["ones_r"])
    for i, v in enumerate((1e-5, -math.log(THETA) / 64.0, -math.pi, ISQ,
                       1.0 / 128.0, 0.5)):
        ap_ = constv[:, i:i + 1]
        p.emit("pool", lambda h, ap_=ap_, v=v: h.memset(ap_, v),
               writes=["constv"])
        nc.const_aps.register(F32, v, ap_)

    emit_mm_group([(PA[:, 0:32], ones_r[:], conv_row[:], True, True)],
                  reads=["ones_r", "conv_row"], writes=["PA"], tag="bcconv")
    p.emit("act", lambda h: h.activation(convb[:], PA[:, 0:32], AF.Copy),
           reads=["PA"], writes=["convb"])
    emit_mm_group([(PB[:, 0:4], ones_r[:], gz_row[:], True, True)],
                  reads=["ones_r", "gz_row"], writes=["PB"], tag="bcgz")
    p.emit("act", lambda h: h.activation(gzb[:], PB[:, 0:4], AF.Copy),
           reads=["PB"], writes=["gzb"])

    xv = xres[:].rearrange("p (fc t) -> p fc t", t=T)
    for tt in range(8):
        stg = Afv(STG[tt % 2], 2048)
        srg = f"stg{tt % 2}"
        p.emit("sync", lambda h, tt=tt, stg=stg: h.dma_start(
            out=stg, in_=x_in[tt * 128:(tt + 1) * 128, :]),
            writes=[srg], sem=f"dsx{tt % 2}", amt=16)
        items_a, items_b = [], []
        for fc in range(KC):
            dst = PA if fc < 8 else PB
            slot = fc % 8
            (items_a if fc < 8 else items_b).append(
                (dst[:, slot * 128:(slot + 1) * 128],
                 stg[:, fc * 128:(fc + 1) * 128], identf[:]))
        emit_tr_group(items_a, reads=[srg, "identf"], writes=["PA"])
        emit_tr_group(items_b, reads=[srg, "identf"], writes=["PB"])
        p.emit("dve", lambda h, tt=tt: h.tensor_copy(
            xv[:, 0:8, tt * 128:(tt + 1) * 128],
            PA[:].rearrange("p (a b) -> p a b", b=128)),
            reads=["PA"], writes=[f"x{fc}" for fc in range(8)])
        p.emit("dve", lambda h, tt=tt: h.tensor_copy(
            xv[:, 8:16, tt * 128:(tt + 1) * 128],
            PB[:].rearrange("p (a b) -> p a b", b=128)),
            reads=["PB"], writes=[f"x{fc}" for fc in range(8, 16)])

    def emit_ln(tp):
        lnf = Afv(LNF_O, 6144)
        mB = lnf[:, 0:1024]
        rB = lnf[:, 1024:2048]
        mrow = lnf[0:1, 2048:3072]
        srow = lnf[0:1, 3072:4096]
        trow = lnf[0:1, 4096:5120]
        tmp0 = lnf[:, 5120:6144]
        tmp1 = C1[:, 0:1024]
        sq = C1b[:, 0:1024]
        for fc in range(KC):
            p.emit("dve", lambda h, fc=fc: h.tensor_copy(
                xn[:, fc * T:(fc + 1) * T], xres[:, fc * T:(fc + 1) * T]),
                reads=[f"x{fc}"], writes=[f"n{fc}"], tag=f"{tp}cast{fc}")
        for half in range(2):
            items = [(PA[0:1, half * 512:(half + 1) * 512], ones_c[:],
                      xn[:, fc * T + half * 512: fc * T + half * 512 + 512],
                      fc == 0, fc == KC - 1) for fc in range(KC)]
            emit_mm_group(items, reads=["ones_c"] + NREG, writes=["PA"],
                          tag=f"{tp}sum{half}")
        for fc in range(KC):
            p.emit("dve", lambda h, fc=fc: h.tensor_tensor(
                sq, xn[:, fc * T:(fc + 1) * T], xn[:, fc * T:(fc + 1) * T],
                op=ALU.mult),
                reads=[f"n{fc}"], writes=["c1a"], tag=f"{tp}sq{fc}")
            items = [(PB[0:1, half * 512:(half + 1) * 512], ones_c[:],
                      sq[:, half * 512:half * 512 + 512], fc == 0, fc == KC - 1)
                     for half in range(2)]
            emit_mm_group(items, reads=["ones_c", "c1a"], writes=["PB"],
                          tag=f"{tp}ssq{fc}")
        p.emit("dve", lambda h: h.tensor_scalar(
            mrow, PA[0:1, :], 1.0 / D, None, op0=ALU.mult),
            reads=["PA"], writes=["mrow"])
        p.emit("dve", lambda h: h.tensor_scalar(
            srow, PB[0:1, :], 1.0 / D, None, op0=ALU.mult),
            reads=["PB"], writes=["srow"])
        p.emit("dve", lambda h: h.tensor_tensor(trow, mrow, mrow, op=ALU.mult),
               reads=["mrow"], writes=["trow"])
        p.emit("dve", lambda h: h.tensor_tensor(srow, srow, trow, op=ALU.subtract),
               reads=["srow", "trow"], writes=["srow"])
        p.emit("act", lambda h: h.activation(trow, srow, AF.Sqrt, bias=1e-5),
               reads=["srow"], writes=["trow"])
        p.emit("dve", lambda h: h.reciprocal(srow, trow),
               reads=["trow"], writes=["srow"])
        emit_mm_group([(PC[:, 0:512], ones_r[:], mrow[0:1, 0:512], True, True),
                       (PC[:, 512:1024], ones_r[:], mrow[0:1, 512:1024], True, True)],
                      reads=["ones_r", "mrow"], writes=["PC"], tag=tp + "bcm")
        emit_mm_group([(PD[:, 0:512], ones_r[:], srow[0:1, 0:512], True, True),
                       (PD[:, 512:1024], ones_r[:], srow[0:1, 512:1024], True, True)],
                      reads=["ones_r", "srow"], writes=["PD"], tag=tp + "bcr")
        p.emit("act", lambda h: h.activation(mB, PC[:], AF.Copy),
               reads=["PC"], writes=["mB"])
        p.emit("act", lambda h: h.activation(rB, PD[:], AF.Copy),
               reads=["PD"], writes=["rB"])
        for fc in range(KC):
            tm = tmp0 if fc % 2 == 0 else tmp1
            trg = "lntmp" if fc % 2 == 0 else "c1a"
            p.emit("dve", lambda h, fc=fc, tm=tm: h.tensor_tensor(
                tm, xres[:, fc * T:(fc + 1) * T], mB, op=ALU.subtract),
                reads=[f"x{fc}", "mB"], writes=[trg], tag=f"{tp}sub{fc}")
            p.emit("dve", lambda h, fc=fc, tm=tm: h.tensor_tensor(
                xn[:, fc * T:(fc + 1) * T], tm, rB, op=ALU.mult),
                reads=[trg, "rB"], writes=[f"n{fc}"], tag=f"{tp}mul{fc}")

    def emit_ffn(rnd):
        tp = f"f{rnd}_"

        def load_g(cb):
            slot = cb % 2
            dst = Ab(GSLOT[slot], 4096)
            p.emit("pool", lambda h, cb=cb, dst=dst: h.dma_start(
                out=dst.rearrange("p (kc c) -> p kc c", c=256), in_=wg_d[cb]),
                writes=[f"g{slot}"], sem=f"wg{slot}", amt=16, tag=f"{tp}ldg{cb}")

        def load_u(cb):
            slot = cb % 2
            dst = Ab(USLOT[slot], 4096)
            p.emit("pool", lambda h, cb=cb, dst=dst: h.dma_start(
                out=dst.rearrange("p (kc c) -> p kc c", c=256), in_=wu_d[cb]),
                writes=[f"u{slot}"], sem=f"wu{slot}", amt=16, tag=f"{tp}ldu{cb}")

        def load_wd(qd):
            if qd >= 32:
                return
            slot = qd % 3
            dst = Ab(WDSLOT[slot], 4096)
            p.emit("pool", lambda h, qd=qd, dst=dst: h.dma_start(
                out=dst.rearrange("p (ic c) -> p ic c", c=512), in_=wd_d[qd]),
                writes=[f"wd{slot}"], sem=f"wd{slot}", amt=16, tag=f"{tp}ldd{qd}")

        def gate_up_chunk(cc):
            cb, m2 = cc // 2, cc % 2
            gs = Ab(GSLOT[cb % 2], 4096)
            us = Ab(USLOT[cb % 2], 4096)
            for w, ps, nm, wrg in ((gs, PA, "PA", f"g{cb % 2}"),
                                   (us, PB, "PB", f"u{cb % 2}")):
                items = []
                for kc in range(KC):
                    for nb in range(2):
                        items.append((ps[:, nb * 512:(nb + 1) * 512],
                                      w[:, kc * 256 + m2 * 128: kc * 256 + (m2 + 1) * 128],
                                      xn[:, kc * T + nb * 512: kc * T + nb * 512 + 512],
                                      kc == 0, kc == KC - 1))
                emit_mm_group(items, reads=[wrg] + NREG, writes=[nm],
                              tag=f"{tp}gu{cc}")
            sg = C1[:, 0:1024]
            p.emit("act", lambda h, sg=sg: h.activation(sg, PA[:], AF.Silu),
                   reads=["PA"], writes=["c1a"], tag=f"{tp}silu{cc}")
            slot, pos = (cc // 4) % 3, cc % 4
            hdst = Ab(HSPSLOT[slot] + pos * 1024, 1024)
            p.emit("dve", lambda h, hdst=hdst, sg=sg: h.tensor_tensor(
                hdst, sg, PB[:], op=ALU.mult),
                reads=["c1a", "PB"], writes=[f"hsp{slot}_{pos}"],
                tag=f"{tp}hmul{cc}")

        def sparsify_group(gs):
            slot = gs % 3
            hts = [Ab(HSPSLOT[slot] + i * 1024, 1024) for i in range(4)]
            ab = [Ab(ABS_O + i * 1024, 1024) for i in range(4)]
            wk = [Ab(WK_O + i * 1024, 1024) for i in range(3)]
            hr = [f"hsp{slot}_{i}" for i in range(4)]
            for i in range(4):
                p.emit("act", lambda h, a=ab[i], t=hts[i]: h.activation(
                    a, t, AF.Abs),
                    reads=[hr[i]], writes=[f"ab{i}"], tag=f"{tp}abs{gs}_{i}")
            net = [
                (wk[0], ab[0], ab[1], ALU.max, ["ab0", "ab1"], "wk0"),
                (wk[1], ab[2], ab[3], ALU.max, ["ab2", "ab3"], "wk1"),
                (wk[0], wk[0], wk[1], ALU.min, ["wk0", "wk1"], "wk0"),
                (wk[1], ab[0], ab[1], ALU.min, ["ab0", "ab1"], "wk1"),
                (wk[2], ab[2], ab[3], ALU.min, ["ab2", "ab3"], "wk2"),
                (wk[1], wk[1], wk[2], ALU.max, ["wk1", "wk2"], "wk1"),
                (wk[0], wk[0], wk[1], ALU.max, ["wk0", "wk1"], "wk0"),
            ]
            for o, i0, i1, op, rd, wr in net:
                p.emit("dve", lambda h, o=o, i0=i0, i1=i1, op=op: h.tensor_tensor(
                    o, i0, i1, op=op), reads=rd, writes=[wr], tag=f"{tp}net{gs}")
            for i in range(4):
                p.emit("dve", lambda h, a=ab[i]: h.tensor_tensor(
                    a, a, wk[0], op=ALU.is_ge),
                    reads=[f"ab{i}", "wk0"], writes=[f"ab{i}"], tag=f"{tp}ge{gs}")
                p.emit("dve", lambda h, t=hts[i], a=ab[i]: h.tensor_tensor(
                    t, t, a, op=ALU.mult),
                    reads=[hr[i], f"ab{i}"], writes=[hr[i]], tag=f"{tp}msk{gs}")

        def down_sweep(g8):
            s0, s1 = (2 * g8) % 3, (2 * g8 + 1) % 3
            for fc in range(KC):
                qd = 4 * g8 + fc // 4
                wdsl = Ab(WDSLOT[qd % 3], 4096)
                fo = (fc % 4) * 128
                ps, nm = (PC, "PC") if fc % 2 == 0 else (PD, "PD")
                items, hreads = [], []
                for ic in range(8):
                    slot = s0 if ic < 4 else s1
                    pos = ic % 4
                    hsp = Ab(HSPSLOT[slot] + pos * 1024, 1024)
                    hreads.append(f"hsp{slot}_{pos}")
                    for nb in range(2):
                        items.append((ps[:, nb * 512:(nb + 1) * 512],
                                      wdsl[:, ic * 512 + fo: ic * 512 + fo + 128],
                                      hsp[:, nb * 512:nb * 512 + 512],
                                      ic == 0, ic == 7))
                emit_mm_group(items, reads=[f"wd{qd % 3}"] + hreads,
                              writes=[nm], tag=f"{tp}dn{g8}_{fc}")
                p.emit("dve", lambda h, fc=fc, ps=ps: h.scalar_tensor_tensor(
                    xres[:, fc * T:(fc + 1) * T], ps[:], 0.5,
                    xres[:, fc * T:(fc + 1) * T], op0=ALU.mult, op1=ALU.add),
                    reads=[nm, f"x{fc}"], writes=[f"x{fc}"], tag=f"{tp}acc{g8}_{fc}")
                if fc % 4 == 3:
                    load_wd(qd + 3)

        load_g(0); load_u(0); load_g(1); load_u(1)
        load_wd(0); load_wd(1); load_wd(2)
        for cb in range(32):
            gate_up_chunk(2 * cb)
            gate_up_chunk(2 * cb + 1)
            if cb % 2 == 1:
                sparsify_group(cb // 2)
                gs_ = cb // 2
                if rnd == 0:
                    raw_dump(f"raw_g{gs_ // 4}",
                             Ab(HSPSLOT[gs_ % 3], 4096),
                             [f"hsp{gs_ % 3}_{i}" for i in range(4)],
                             col0=(gs_ % 4) * 4096)
                if cb == 1 and rnd == 0:
                    raw_dump("raw_hsp0", Ab(HSPSLOT[0], 4096),
                             [f"hsp0_{i}" for i in range(4)])
                if cb == 3 and rnd == 0:
                    raw_dump("raw_hsp1", Ab(HSPSLOT[1], 4096),
                             [f"hsp1_{i}" for i in range(4)])
                    raw_dump("raw_wd0", Ab(WDSLOT[0], 4096), ["wd0"])
            if cb + 2 < 32:
                load_g(cb + 2)
                load_u(cb + 2)
            if cb >= 4 and (cb - 4) % 4 == 0:
                g8_ = (cb - 4) // 4
                if g8_ == 1 and rnd == 0:
                    raw_dump("raw_s2", Ab(HSPSLOT[2], 4096),
                             [f"hsp2_{i}" for i in range(4)])
                    raw_dump("raw_s0b", Ab(HSPSLOT[0], 4096),
                             [f"hsp0_{i}" for i in range(4)])
                if g8_ == 3 and rnd == 0:
                    raw_dump("raw_wq0", Ab(WDSLOT[0], 4096), ["wd0"])
                    raw_dump("raw_wq1", Ab(WDSLOT[1], 4096), ["wd1"])
                    raw_dump("raw_wq2", Ab(WDSLOT[2], 4096), ["wd2"])
                    raw_dump("raw_hs0", Ab(HSPSLOT[0], 4096),
                             [f"hsp0_{i}" for i in range(4)])
                    raw_dump("raw_hs1", Ab(HSPSLOT[1], 4096),
                             [f"hsp1_{i}" for i in range(4)])
                    raw_dump("raw_hs2", Ab(HSPSLOT[2], 4096),
                             [f"hsp2_{i}" for i in range(4)])
                down_sweep(g8_)
                if cb == 4 and rnd == 0:
                    raw_dump("raw_xg0", xres[:],
                             [f"x{fc}" for fc in range(KC)])
                if cb == 8 and rnd == 0:
                    raw_dump("raw_xg1", xres[:],
                             [f"x{fc}" for fc in range(KC)])
        down_sweep(7)

    def emit_tables():
        cosT = Afv(COS_O, 1024)
        sinT = Afv(COS_O + 2048, 1024)
        inv = C1[0:1, 0:128]
        pos = C1[0:1, 128:1152]
        p.emit("pool", lambda h: h.iota(
            inv, pattern=[[0, 2], [1, 64]], base=0, channel_multiplier=0,
            allow_small_or_imprecise_dtypes=True), writes=["tt2"])
        p.emit("pool", lambda h: h.iota(
            pos, pattern=[[1, 1024]], base=0, channel_multiplier=0,
            allow_small_or_imprecise_dtypes=True), writes=["c1a", "c1hi"])
        p.emit("act", lambda h: h.activation(
            inv, inv, AF.Exp, scale=-math.log(THETA) / 64.0),
            reads=["tt2"], writes=["tt2"])
        p.emit("dve", lambda h: h.tensor_scalar(
            pos, pos, gz_row[0:1, 2:3], None, op0=ALU.add),
            reads=["c1a", "c1hi", "gz_row"], writes=["c1a", "c1hi"])
        emit_mm_group([(PC[:, 0:512], inv, pos[0:1, 0:512], True, True),
                       (PC[:, 512:1024], inv, pos[0:1, 512:1024], True, True)],
                      reads=["tt2", "c1a", "c1hi"], writes=["PC"], tag="angles")
        twopi = 2 * math.pi
        mtmp = C1[:, 512:1536]
        mint = C1[:, 0:1024].bitcast(I32)[:, 0:1024]
        for dst, drg, phase in ((sinT, "sinT", 0.0), (cosT, "cosT", math.pi / 2)):
            p.emit("dve", lambda h, dst=dst, phase=phase: h.tensor_scalar(
                dst, PC[:], phase, None, op0=ALU.add),
                reads=["PC"], writes=[drg])
            p.emit("dve", lambda h, dst=dst: h.tensor_scalar(
                mtmp, dst, 1.0 / twopi, None, op0=ALU.mult),
                reads=[drg], writes=["c1hi"])
            p.emit("dve", lambda h: h.tensor_copy(mint, mtmp),
                   reads=["c1hi"], writes=["c1a"])
            p.emit("dve", lambda h: h.tensor_copy(mtmp, mint),
                   reads=["c1a"], writes=["c1hi"])
            p.emit("dve", lambda h, dst=dst: h.scalar_tensor_tensor(
                dst, mtmp, -twopi, dst, op0=ALU.mult, op1=ALU.add),
                reads=["c1hi", drg], writes=[drg])
            p.emit("dve", lambda h, dst=dst: h.tensor_scalar(
                mtmp, dst, math.pi, None, op0=ALU.is_gt),
                reads=[drg], writes=["c1hi"])
            p.emit("dve", lambda h, dst=dst: h.scalar_tensor_tensor(
                dst, mtmp, -twopi, dst, op0=ALU.mult, op1=ALU.add),
                reads=["c1hi", drg], writes=[drg])
            p.emit("act", lambda h, dst=dst: h.activation(dst, dst, AF.Sin),
                   reads=[drg], writes=[drg])

    def emit_attention():
        cosT = Afv(COS_O, 1024)
        sinT = Afv(COS_O + 2048, 1024)
        kraw = C1b[:, 2048:3072]
        tt2 = C1[:, 0:512]

        emit_tables()

        wp_s = Ab(WP_O, 8192)
        for pan in range(4):
            p.emit("pool", lambda h, pan=pan: h.dma_start(
                out=wp_s.rearrange("p (kc c) -> p kc c", c=512), in_=wkv_d[pan]),
                writes=["wwp"], sem="wwp", amt=16, tag=f"ldwkv{pan}")
            for sub in range(4):
                ch = pan * 4 + sub
                ps, nm = (PA, "PA") if ch % 2 == 0 else (PB, "PB")
                items = []
                for kc in range(KC):
                    for nb in range(2):
                        items.append((ps[:, nb * 512:(nb + 1) * 512],
                                      wp_s[:, kc * 512 + sub * 128: kc * 512 + (sub + 1) * 128],
                                      xn[:, kc * T + nb * 512: kc * T + nb * 512 + 512],
                                      kc == 0, kc == KC - 1))
                emit_mm_group(items, reads=["wwp"] + NREG, writes=[nm],
                              tag=f"kv{ch}")
                if ch < 8:
                    j = ch
                    kown = Ab(KSLOT[j] + 1024, 1024)
                    p.emit("act", lambda h, ps=ps: h.activation(
                        kraw, ps[:], AF.Copy),
                        reads=[nm], writes=["c1hi"], tag=f"kevac{j}")
                    for nb in range(2):
                        emit_mm_group([(PD[:, nb * 512:(nb + 1) * 512], swapm[:],
                                        kraw[:, nb * 512:nb * 512 + 512], True, True)],
                                      reads=["c1hi", "swapm"], writes=["PD"],
                                      tag=f"kswap{j}")
                        p.emit("dve", lambda h, nb=nb: h.tensor_tensor(
                            PD[:, nb * 512:(nb + 1) * 512],
                            PD[:, nb * 512:(nb + 1) * 512],
                            sinT[:, nb * 512:nb * 512 + 512], op=ALU.mult),
                            reads=["PD", "sinT"], writes=["PD"], tag=f"ksin{j}")
                        p.emit("dve", lambda h, nb=nb: h.tensor_tensor(
                            tt2, kraw[:, nb * 512:nb * 512 + 512],
                            cosT[:, nb * 512:nb * 512 + 512], op=ALU.mult),
                            reads=["c1hi", "cosT"], writes=["tt2"], tag=f"kcos{j}")
                        p.emit("dve", lambda h, kown=kown, nb=nb: h.tensor_tensor(
                            kown[:, nb * 512:nb * 512 + 512],
                            PD[:, nb * 512:(nb + 1) * 512], tt2, op=ALU.add),
                            reads=["PD", "tt2"], writes=[f"kown{j}"], tag=f"krot{j}")
                    p.emit("sync", lambda h, j=j, kown=kown: h.dma_start(
                        out=cc_in[j * 128:(j + 1) * 128, :], in_=kown),
                        reads=[f"kown{j}"], writes=[f"ccin{j}"], sem="dkv", amt=16,
                        tag=f"ctk{j}")
                else:
                    j = ch - 8
                    vown = Ab(VSLOT[j] + 1024, 1024)
                    p.emit("act", lambda h, ps=ps, vown=vown: h.activation(
                        vown, ps[:], AF.Copy),
                        reads=[nm], writes=[f"vown{j}"], tag=f"vevac{j}")
                    p.emit("sync", lambda h, j=j, vown=vown: h.dma_start(
                        out=cc_in[1024 + j * 128: 1024 + (j + 1) * 128, :],
                        in_=vown),
                        reads=[f"vown{j}"], writes=[f"ccinv{j}"], sem="dkv", amt=16,
                        tag=f"ctv{j}")

        p.emit("pool", lambda h: h.collective_compute(
            "AllGather", ALU.bypass, replica_groups=[list(range(NCORES))],
            ins=[cc_in.opt()], outs=[cc_out.opt()]),
            reads=[f"ccin{j}" for j in range(8)] + [f"ccinv{j}" for j in range(8)],
            writes=["ccout"], sem="cc", amt=1, tag="allgather")
        for j in range(8):
            p.emit("pool", lambda h, j=j: h.indirect_dma_start(
                out=Ab(KSLOT[j], 1024), out_offset=None, in_=cc_out,
                in_offset=bass.IndirectOffsetOnAxis(ap=idx_sb[:, j:j + 1], axis=0)),
                reads=["ccout", "idx"], writes=[f"khalo{j}"], sem="hg", amt=16,
                tag=f"gk{j}")
            p.emit("pool", lambda h, j=j: h.indirect_dma_start(
                out=Ab(VSLOT[j], 1024), out_offset=None, in_=cc_out,
                in_offset=bass.IndirectOffsetOnAxis(ap=idx_sb[:, 8 + j:9 + j],
                                                    axis=0)),
                reads=["ccout", "idx"], writes=[f"vhalo{j}"], sem="hg", amt=16,
                tag=f"gv{j}")

        for j in range(7, -1, -1):
            for base, ow, hw, w0c, w1c in ((KSLOT, "kown", "khalo", 0, 8),
                                           (VSLOT, "vown", "vhalo", 16, 24)):
                src = Ab(base[j], 2048)
                dst = Ab(base[j + 1], 2048)
                rd = [f"{ow}{j}", f"{hw}{j}"]
                wdst = [f"{ow}{j}_c"] + ([f"{ow}{j + 1}", f"{hw}{j + 1}"]
                                         if j < 7 else [])
                p.emit("dve", lambda h, src=src: h.tensor_scalar(
                    src[:, 1023:1024], src[:, 1023:1024], gzb[:, 0:1], None,
                    op0=ALU.mult),
                    reads=[rd[1], "gzb"], writes=[rd[1]], tag=f"gz{j}")
                p.emit("dve", lambda h, src=src, dst=dst, c=w1c + j:
                       h.tensor_scalar(dst, src, convb[:, c:c + 1], None,
                                       op0=ALU.mult),
                       reads=rd + ["convb"], writes=wdst, tag=f"cw1{j}")
                p.emit("dve", lambda h, src=src, dst=dst, c=w0c + j:
                       h.scalar_tensor_tensor(
                           dst[:, 1:2048], src[:, 0:2047], convb[:, c:c + 1],
                           dst[:, 1:2048], op0=ALU.mult, op1=ALU.add),
                       reads=rd + ["convb", f"{ow}{j}_c"],
                       writes=[f"{ow}{j}_c"], tag=f"cw0{j}")

        for j in range(8):
            vsl = Ab(VSLOT[j + 1], 2048)
            for half in range(2):
                items = []
                for t8 in range(8):
                    kt = half * 8 + t8
                    items.append((PCb[:, half * 1024 + t8 * 128:
                                      half * 1024 + (t8 + 1) * 128],
                                  vsl[:, kt * 128:(kt + 1) * 128], identb[:]))
                emit_tr_group(items, reads=[f"vown{j}_c", "identb"],
                              writes=["PC"], tag=f"vt{j}_{half}")
                p.emit("dve", lambda h, vsl=vsl, half=half: h.tensor_copy(
                    vsl[:, half * 1024:(half + 1) * 1024],
                    PCb[:, half * 1024:(half + 1) * 1024]),
                    reads=["PC"],
                    writes=[f"vT{j}"] + ([f"vown{j}_c"] if half == 1 else []),
                    tag=f"vte{j}_{half}")

        p.mark_alias(["wwq", "q0", "q1", "P0", "P1", "PT"], ["wwp"])
        p.mark_alias(["PA", "PB2", "PB3", "PC", "PD6", "PD7"],
                     ["PA", "PB", "PC", "PD", "PD2", "PBlo", "PBhi"])
        Pbs = [Ab(40960, 1536), Ab(42496, 1536)]
        PTs = Ab(44032, 1024)
        rsumB = C1[:, 512:1536]
        rr2 = C1[0:1, 0:1024]
        wq_s = Ab(WQ_O, 2048)
        sqscr = Ab(WQ_O, 1024)
        GOFF = (0, 128, 256, 512, 640, 768, 1024, 1152, 1280)
        SSEG = [([PCD[:, 0:384], PCD[:, 512:896], PCD[:, 1024:1408]],
                 ["PC", "PC", "PD6"], PCD[:, 0:1536]),
                ([PAB[:, 0:384], PAB[:, 512:896], PAB[:, 1024:1408]],
                 ["PA", "PA", "PB2"], PAB[:, 0:1536])]
        PTP = PABb[:, 3072:4096]
        PVAP = PCD[:, 1536:1664]
        SUMAP = PCD[0:1, 1664:1792]
        PT9P = PCDb[:, 3584:3712]
        QSQR = PCD[0:1, 1536:2048]
        MFMP = PCD[:, 1856:1864]

        def emit_qhead(hh):
            qslot = Ab(Q_O[hh % 2], 1024)
            qrg = f"q{hh % 2}"
            kv2 = hh // 2
            p.emit("pool", lambda h, hh=hh: h.dma_start(
                out=wq_s.rearrange("p (kc c) -> p kc c", c=128), in_=wq_d[hh]),
                writes=["wwq"], sem="wwq", amt=16, tag=f"ldwq{hh}")
            items = []
            for kc in range(KC):
                for nb in range(2):
                    items.append((PA[:, nb * 512:(nb + 1) * 512],
                                  wq_s[:, kc * 128:(kc + 1) * 128],
                                  xn[:, kc * T + nb * 512: kc * T + nb * 512 + 512],
                                  kc == 0, kc == KC - 1))
            emit_mm_group(items, reads=["wwq"] + NREG, writes=["PA"],
                          tag=f"qmm{hh}")
            p.emit("act", lambda h, qslot=qslot: h.activation(
                qslot, PA[:], AF.Copy), reads=["PA"], writes=[qrg],
                tag=f"qevac{hh}")
            p.emit("dve", lambda h: h.memset(PAB[:, 384:512], NEG),
                   reads=[], writes=["PA"], tag=f"gm1{hh}")
            p.emit("dve", lambda h: h.memset(PAB[:, 896:1024], NEG),
                   reads=[], writes=["PA"], tag=f"gm2{hh}")
            for nb in range(2):
                emit_mm_group([(PCD[:, 1536:2048], swapm[:],
                                qslot[:, nb * 512:nb * 512 + 512], True, True)],
                              reads=[qrg, "swapm"], writes=["PD7"],
                              tag=f"qswap{hh}")
                p.emit("dve", lambda h, nb=nb: h.tensor_tensor(
                    PCD[:, 1536:2048], PCD[:, 1536:2048],
                    sinT[:, nb * 512:nb * 512 + 512], op=ALU.mult),
                    reads=["PD7", "sinT"], writes=["PD7"], tag=f"qsin{hh}")
                p.emit("dve", lambda h, qslot=qslot, nb=nb: h.tensor_tensor(
                    tt2, qslot[:, nb * 512:nb * 512 + 512],
                    cosT[:, nb * 512:nb * 512 + 512], op=ALU.mult),
                    reads=[qrg, "cosT"], writes=["tt2"], tag=f"qcos{hh}")
                p.emit("dve", lambda h, qslot=qslot, nb=nb: h.tensor_tensor(
                    qslot[:, nb * 512:nb * 512 + 512],
                    PCD[:, 1536:2048], tt2, op=ALU.add),
                    reads=["PD7", "tt2"], writes=[qrg], tag=f"qrot{hh}")
            p.emit("act", lambda h, qslot=qslot: h.activation(
                sqscr, qslot, AF.Square),
                reads=[qrg], writes=["wwq"], tag=f"qsq{hh}")
            for half in range(2):
                emit_mm_group([(QSQR[0:1, 0:512], ones_c[:],
                                sqscr[:, half * 512:(half + 1) * 512],
                                True, True)],
                              reads=["wwq", "ones_c"], writes=["PD7"],
                              tag=f"qsm{hh}_{half}")
                p.emit("dve", lambda h, half=half, kv2=kv2: h.tensor_scalar(
                    mrow_s[0:1, half * 512:(half + 1) * 512], QSQR[0:1, 0:512],
                    kms[0:1, kv2:kv2 + 1], None, op0=ALU.mult),
                    reads=["PD7", "kms"], writes=["mrow"], tag=f"qmr{hh}_{half}")
            items = [(MFMP[:, a:a + 1], mrow_s[0:1, a * 128:(a + 1) * 128],
                      ones_r[0:1, 0:1], True, True) for a in range(8)]
            emit_mm_group(items, reads=["mrow", "ones_r"], writes=["PD7"],
                          tag=f"mfm{hh}")
            p.emit("act", lambda h, hh=hh: h.activation(
                mfm[hh % 2][:], MFMP, AF.Ln, scale=1.0 / 128.0),
                reads=["PD7"], writes=[f"mfm{hh % 2}"], tag=f"mfl{hh}")
            p.emit("act", lambda h, hh=hh: h.activation(
                mfm[hh % 2][:], mfm[hh % 2][:], AF.Exp, scale=0.5),
                reads=[f"mfm{hh % 2}"], writes=[f"mfm{hh % 2}"], tag=f"mfs{hh}")
            p.emit("dve", lambda h, hh=hh: h.tensor_scalar(
                mfm[hh % 2][:], mfm[hh % 2][:], -1.0, -50.0, op0=ALU.mult,
                op1=ALU.add),
                reads=[f"mfm{hh % 2}"], writes=[f"mfm{hh % 2}"], tag=f"mfn{hh}")

        ksq = Ab(P_O, 2048)
        for j in range(8):
            kc_ap = Ab(KSLOT[j + 1], 2048)
            p.emit("act", lambda h, kc_ap=kc_ap: h.activation(
                ksq, kc_ap, AF.Square),
                reads=[f"kown{j}_c"], writes=["P0", "P1"], tag=f"ksq{j}")
            for half in range(4):
                ps_row = (PAB, "PA") if half < 2 else (PCD, "PC")
                emit_mm_group([(ps_row[0][0:1, (half % 2) * 512:(half % 2) * 512 + 512],
                                ones_c[:], ksq[:, half * 512:(half + 1) * 512],
                                True, True)],
                              reads=["P0", "P1", "ones_c"], writes=[ps_row[1]],
                              tag=f"ksm{j}_{half}")
            p.emit("dve", lambda h: h.tensor_reduce(
                mx1[0:1, 0:1], PAB[0:1, 0:1024], axis=AX.X, op=ALU.max),
                reads=["PA"], writes=["mx1"], tag=f"kr1{j}")
            p.emit("dve", lambda h: h.tensor_reduce(
                mx2[0:1, 0:1], PCD[0:1, 0:1024], axis=AX.X, op=ALU.max),
                reads=["PC"], writes=["mx2"], tag=f"kr2{j}")
            p.emit("dve", lambda h, j=j: h.tensor_tensor(
                kms[0:1, j:j + 1], mx1[0:1, 0:1], mx2[0:1, 0:1], op=ALU.max),
                reads=["mx1", "mx2"], writes=["kms"], tag=f"krc{j}")

        for gap_ap, grg in ((PAB[:, 1408:1536], "PB2"),
                            (PCD[:, 384:512], "PC"),
                            (PCD[:, 896:1024], "PC"),
                            (PCD[:, 1408:1536], "PD6")):
            p.emit("dve", lambda h, gap_ap=gap_ap: h.memset(gap_ap, NEG),
                   reads=[], writes=[grg], tag="gapms")
        if "raw_att" in raw_dumps:
            p.emit("pool", lambda h: h.dma_start(
                out=raw_dumps["raw_att"][0:1, 0:8], in_=kms[0:1, 0:8]),
                reads=["kms"], sem="hg", amt=16, tag="dkms")
        emit_qhead(0)
        if "raw_att" in raw_dumps:
            p.emit("pool", lambda h: h.dma_start(
                out=raw_dumps["raw_att"][:, 16:24], in_=mfm[0][:]),
                reads=["mfm0"], sem="hg", amt=16, tag="dmfm")
            p.emit("pool", lambda h: h.dma_start(
                out=raw_dumps["raw_att"][:, 1024:2048], in_=Ab(Q_O[0], 1024)),
                reads=["q0"], sem="hg", amt=16, tag="dq0")
            p.emit("pool", lambda h: h.dma_start(
                out=raw_dumps["raw_att"][:, 2048:4096], in_=Ab(KSLOT[1], 2048)),
                reads=["kown0_c"], sem="hg", amt=16, tag="dk0")
            p.emit("pool", lambda h: h.dma_start(
                out=raw_dumps["raw_att"][0:1, 4096:5120], in_=mrow_s[:]),
                reads=["mrow"], sem="hg", amt=16, tag="dmr")
        for h_ in range(NH):
            kv = h_ // 2
            ksl = Ab(KSLOT[kv + 1], 2048)
            vsl = Ab(VSLOT[kv + 1], 2048)
            qslot = Ab(Q_O[h_ % 2], 1024)
            qrg = f"q{h_ % 2}"
            mrg = f"mfm{h_ % 2}"
            mf = mfm[h_ % 2]

            def emit_S(a, h_=h_, kv=kv, ksl=ksl, qslot=qslot, qrg=qrg):
                segs, rgs, _ = SSEG[a % 2]
                for si in range(3):
                    kcols = 128 * a + si * 384
                    emit_mm_group([(segs[si], qslot[:, a * 128:(a + 1) * 128],
                                    ksl[:, kcols:kcols + 384], True, True)],
                                  reads=[qrg, f"kown{kv}_c"], writes=[rgs[si]],
                                  tag=f"s{h_}_{a}_{si}")

            def emit_soft(a, h_=h_, mf=mf, mrg=mrg):
                segs, rgs, full = SSEG[a % 2]
                Pb = Pbs[a % 2]
                prg = f"P{a % 2}"
                p.emit("act", lambda h, full=full, Pb=Pb, a=a: h.activation(
                    Pb, full, AF.Exp, bias=mf[:, a:a + 1], scale=ISQ),
                    reads=rgs + [mrg], writes=[prg], tag=f"exp{h_}_{a}")
                z = 128 * (8 - a)
                zg = z + (128 if z > 384 else 0) + (128 if z > 768 else 0)
                p.emit("dve", lambda h, zg=zg, Pb=Pb: h.tensor_scalar(
                    Pb[:, 0:zg], Pb[:, 0:zg], gzb[:, 0:1], None, op0=ALU.mult),
                    reads=[prg, "gzb"], writes=[prg], tag=f"z{h_}_{a}")
                p.emit("dve", lambda h, Pb=Pb: h.tensor_tensor(
                    Pb[:, 0:128], Pb[:, 0:128], oldM[:], op=ALU.mult),
                    reads=[prg, "oldM"], writes=[prg], tag=f"om{h_}_{a}")
                p.emit("dve", lambda h, Pb=Pb: h.tensor_tensor(
                    Pb[:, 1280:1408], Pb[:, 1280:1408], diagM[:], op=ALU.mult),
                    reads=[prg, "diagM"], writes=[prg], tag=f"dm{h_}_{a}")

            def emit_tail(a, h_=h_, kv=kv, vsl=vsl):
                Pb = Pbs[a % 2]
                prg = f"P{a % 2}"
                items = [(PTP[:, i * 128:(i + 1) * 128],
                          Pb[:, GOFF[i]:GOFF[i] + 128], identb[:])
                         for i in range(8)]
                emit_tr_group(items, reads=[prg, "identb"], writes=["PB3"],
                              tag=f"pt{h_}_{a}")
                emit_tr_group([(PT9P, Pb[:, 1280:1408], identb[:])],
                              reads=[prg, "identb"], writes=["PD7"],
                              tag=f"pt9{h_}_{a}")
                p.emit("act", lambda h: h.activation(
                    PTs[:, 0:1024], PTP, AF.Copy),
                    reads=["PB3"], writes=["PT"], tag=f"pte{h_}_{a}")
                p.emit("act", lambda h: h.activation(pt9s[:], PT9P, AF.Copy),
                       reads=["PD7"], writes=["pt9s"], tag=f"pt9e{h_}_{a}")
                items = [(PVAP, vsl[:, (a + i) * 128:(a + i + 1) * 128],
                          PTs[:, i * 128:(i + 1) * 128] if i < 8 else pt9s[:],
                          i == 0, i == 8)
                         for i in range(9)]
                emit_mm_group(items, reads=["PT", "pt9s", f"vT{kv}"],
                              writes=["PD7"], tag=f"pv{h_}_{a}")
                items = [(SUMAP, ones_c[:],
                          PTs[:, i * 128:(i + 1) * 128] if i < 8 else pt9s[:],
                          i == 0, i == 8)
                         for i in range(9)]
                emit_mm_group(items, reads=["PT", "pt9s", "ones_c"],
                              writes=["PD7"], tag=f"sum{h_}_{a}")
                ao = Ab(KSLOT[h_ // 2] + (h_ % 2) * 1024, 1024)
                p.emit("dve", lambda h, ao=ao, a=a: h.tensor_copy(
                    ao[:, a * 128:(a + 1) * 128], PVAP),
                    reads=["PD7"], writes=[f"ao{h_}"], tag=f"aoe{h_}_{a}")
                p.emit("dve", lambda h, a=a: h.reciprocal(
                    rr2[0:1, a * 128:(a + 1) * 128], SUMAP),
                    reads=["PD7"], writes=["tt2", "c1a"], tag=f"rc{h_}_{a}")

            emit_S(0)
            if h_ + 1 < NH:
                emit_qhead(h_ + 1)
            emit_S(1)
            for a in range(1, 9):
                if a < 8:
                    if a >= 2:
                        emit_S(a)
                    emit_soft(a - 1)
                    emit_tail(a - 1)
                else:
                    emit_soft(7)
                    emit_tail(7)
            items = [(PA[:, a * 128:(a + 1) * 128], ones_r[:],
                      rr2[0:1, a * 128:(a + 1) * 128], True, True)
                     for a in range(8)]
            emit_mm_group(items, reads=["tt2", "c1a", "ones_r"], writes=["PA"],
                          tag=f"rbc{h_}")
            p.emit("act", lambda h: h.activation(rsumB, PA[:], AF.Copy),
                   reads=["PA"], writes=["c1hi"], tag=f"rbe{h_}")
            ao = Ab(KSLOT[h_ // 2] + (h_ % 2) * 1024, 1024)
            p.emit("dve", lambda h, ao=ao: h.tensor_tensor(
                ao, ao, rsumB, op=ALU.mult),
                reads=[f"ao{h_}", "c1hi"], writes=[f"ao{h_}"], tag=f"aon{h_}")

        p.mark_alias(["wo1"], ["wwq", "q0", "q1", "P0", "P1", "PT"])
        p.mark_alias(["wo0"], ["vT4", "vT5", "vT6", "vT7"])
        p.mark_alias(["PA", "PB", "PC", "PD", "PD2"],
                     ["PA", "PB2", "PB3", "PC", "PD6", "PD7"])
        for pan in range(4):
            slot = pan % 2
            dst = Ab(WO_SLOT[slot], 8192)
            p.emit("pool", lambda h, pan=pan, dst=dst: h.dma_start(
                out=dst.rearrange("p (kc c) -> p kc c", c=512), in_=wo_d[pan]),
                writes=[f"wo{slot}"], sem=f"wo{slot}", amt=16, tag=f"ldwo{pan}")
            for sub in range(4):
                fc = pan * 4 + sub
                ps, nm = (PC, "PC") if fc % 2 == 0 else (PD, "PD")
                items, rds = [], [f"wo{slot}"]
                for hh in range(16):
                    ao = Ab(KSLOT[hh // 2] + (hh % 2) * 1024, 1024)
                    rds.append(f"ao{hh}")
                    for nb in range(2):
                        items.append((ps[:, nb * 512:(nb + 1) * 512],
                                      dst[:, hh * 512 + sub * 128:
                                          hh * 512 + (sub + 1) * 128],
                                      ao[:, nb * 512:nb * 512 + 512],
                                      hh == 0, hh == 15))
                emit_mm_group(items, reads=rds, writes=[nm], tag=f"wom{fc}")
                p.emit("dve", lambda h, fc=fc, ps=ps: h.tensor_tensor(
                    xres[:, fc * T:(fc + 1) * T],
                    xres[:, fc * T:(fc + 1) * T], ps[:], op=ALU.add),
                    reads=[nm, f"x{fc}"], writes=[f"x{fc}"], tag=f"woacc{fc}")

    def emit_store(dram_ap, tag):
        for tt in range(8):
            stg = Afv(OSTG[tt % 2], 2048)
            org = f"ostg{tt % 2}"
            items_a, items_b = [], []
            for fc in range(KC):
                dst = PA if fc < 8 else PB
                slot = fc % 8
                (items_a if fc < 8 else items_b).append(
                    (dst[:, slot * 128:(slot + 1) * 128],
                     xres[:, fc * T + tt * 128: fc * T + (tt + 1) * 128],
                     identf[:]))
            emit_tr_group(items_a,
                          reads=[f"x{fc}" for fc in range(8)] + ["identf"],
                          writes=["PA"], tag=f"{tag}t{tt}a")
            emit_tr_group(items_b,
                          reads=[f"x{fc}" for fc in range(8, 16)] + ["identf"],
                          writes=["PB"], tag=f"{tag}t{tt}b")
            sv = None
            p.emit("dve", lambda h, stg=stg: h.tensor_copy(
                stg.rearrange("p (a b) -> p a b", b=128)[:, 0:8, :],
                PA[:].rearrange("p (a b) -> p a b", b=128)),
                reads=["PA"], writes=[org])
            p.emit("dve", lambda h, stg=stg: h.tensor_copy(
                stg.rearrange("p (a b) -> p a b", b=128)[:, 8:16, :],
                PB[:].rearrange("p (a b) -> p a b", b=128)),
                reads=["PB"], writes=[org])
            p.emit("sync", lambda h, stg=stg, tt=tt, dram_ap=dram_ap:
                   h.dma_start(out=dram_ap[tt * 128:(tt + 1) * 128, :], in_=stg),
                   reads=[org], writes=[], sem=f"dso{tt % 2}", amt=16,
                   tag=f"{tag}st{tt}")

    def raw_dump(name, ap_, rregs, col0=0):
        if name in raw_dumps:
            p.emit("pool", lambda h, ap_=ap_, name=name, col0=col0: h.dma_start(
                out=raw_dumps[name][:, col0:col0 + ap_.shape[-1]], in_=ap_),
                reads=rregs, sem="hg", amt=16, tag="dump" + name)

    p.fence()
    emit_ln("ln1_")
    raw_dump("raw_xn1", xn[:], NREG)
    p.fence()
    emit_ffn(0)
    p.fence()
    raw_dump("raw_x1f", xres[:], [f"x{fc}" for fc in range(KC)])
    if "x1" in dbg_outs:
        emit_store(dbg_outs["x1"], "dbgx1_")
        p.fence()
    emit_ln("ln2_")
    p.fence()
    emit_attention()
    p.fence()
    if "x2" in dbg_outs:
        emit_store(dbg_outs["x2"], "dbgx2_")
        p.fence()
    emit_ln("ln3_")
    p.fence()
    emit_ffn(1)
    p.fence()
    emit_store(y_out, "out_")
    p.raw_wait("sync", "dso0", p.cnt["dso0"])
    p.raw_wait("sync", "dso1", p.cnt["dso1"])

    with nc.Block() as block:
        @block.sync
        def _(h):
            for t_ in p.thunks["sync"]:
                t_(h)

        @block.tensor
        def _(h):
            for t_ in p.thunks["pe"]:
                t_(h)

        @block.vector
        def _(h):
            for t_ in p.thunks["dve"]:
                t_(h)

        @block.scalar
        def _(h):
            for t_ in p.thunks["act"]:
                t_(h)

        @block.gpsimd
        def _(h):
            for t_ in p.thunks["pool"]:
                t_(h)

    es.close()
    p._es.close()
    return nc



def _perm_cols():
    n = np.arange(DI)
    q = n // 512
    r = (n % 512) // 128
    pp = n % 128
    return 4 * (128 * q + pp) + r


def _prep_weights(w_gate, w_up, w_down, w_pack, w_o):
    perm = _perm_cols()
    wg = np.ascontiguousarray(w_gate[:, perm])
    wu = np.ascontiguousarray(w_up[:, perm])
    wd = np.ascontiguousarray(w_down[perm, :])
    wg_d = np.ascontiguousarray(wg.reshape(KC, 128, 32, 256).transpose(2, 1, 0, 3))
    wu_d = np.ascontiguousarray(wu.reshape(KC, 128, 32, 256).transpose(2, 1, 0, 3))
    wd_d = np.ascontiguousarray(
        wd.reshape(8, 8, 128, 4, 512).transpose(0, 3, 2, 1, 4).reshape(32, 128, 8, 512))
    wq_d = np.ascontiguousarray(
        w_pack[:, :2048].reshape(KC, 128, 16, 128).transpose(2, 1, 0, 3))
    wkv_d = np.ascontiguousarray(
        w_pack[:, 2048:].reshape(KC, 128, 4, 512).transpose(2, 1, 0, 3))
    wo_d = np.ascontiguousarray(
        w_o.reshape(KC, 128, 4, 512).transpose(2, 1, 0, 3))
    return wg_d, wu_d, wd_d, wq_d, wkv_d, wo_d


_nc_cache = {}


def kernel(L0_attn_hidden, L0_attn_W_pack, L0_attn_W_o, L0_attn_conv_k,
           L0_attn_conv_v, L0_ffn_w_gate, L0_ffn_w_up, L0_ffn_w_down,
           _debug=()):
    x = np.ascontiguousarray(np.asarray(L0_attn_hidden, dtype=np.float32))
    w_pack = np.asarray(L0_attn_W_pack, dtype=np.float32)
    w_o = np.asarray(L0_attn_W_o, dtype=np.float32)
    conv_k = np.asarray(L0_attn_conv_k, dtype=np.float32)
    conv_v = np.asarray(L0_attn_conv_v, dtype=np.float32)
    w_gate = np.asarray(L0_ffn_w_gate, dtype=np.float32)
    w_up = np.asarray(L0_ffn_w_up, dtype=np.float32)
    w_down = np.asarray(L0_ffn_w_down, dtype=np.float32)

    wg_d, wu_d, wd_d, wq_d, wkv_d, wo_d = _prep_weights(
        w_gate, w_up, w_down, w_pack, w_o)
    conv_flat = np.ascontiguousarray(np.concatenate(
        [conv_k[:, 0], conv_k[:, 1], conv_v[:, 0], conv_v[:, 1]]
    ).reshape(1, 32).astype(np.float32))

    in_maps = []
    for c in range(NCORES):
        b, t0 = c // 4, (c % 4) * T
        xs = np.ascontiguousarray(x[b, t0:t0 + T, :])
        gamma = 1.0 if (c % 4) else 0.0
        zeta = 0.0 if (c % 4) else NEG
        gz = np.array([[gamma, zeta, float(t0), 0.0]], dtype=np.float32)
        rc = c - 1 if (c % 4) else c
        pidx = np.arange(128, dtype=np.int32)
        cols = [rc * 2048 + j * 128 + pidx for j in range(8)]
        cols += [rc * 2048 + 1024 + j * 128 + pidx for j in range(8)]
        idx = np.ascontiguousarray(np.stack(cols, axis=1).astype(np.int32))
        in_maps.append(dict(
            x_in=xs, wg_d=wg_d, wu_d=wu_d, wd_d=wd_d, wq_d=wq_d,
            wkv_d=wkv_d, wo_d=wo_d, conv_in=conv_flat, gz_in=gz, idx_in=idx))

    key = tuple(sorted(_debug))
    if key not in _nc_cache:
        _nc_cache[key] = build_program(debug=_debug)
    nc = _nc_cache[key]

    res = run_bass_kernel_spmd(nc, in_maps, core_ids=list(range(NCORES)))
    shards = [res.results[c]["y_out"] for c in range(NCORES)]
    out = np.stack(shards, axis=0).reshape(2, 4, T, D).reshape(2, 4096, D)
    if _debug:
        dbg = {}
        for name in _debug:
            d = [res.results[c]["dbg_" + name] for c in range(NCORES)]
            dbg[name] = np.stack(d, axis=0).reshape(2, 4096, D)
        return out, dbg
    return out
